# version 03
# baseline (speedup 1.0000x reference)
import numpy as np
from contextlib import ExitStack

import concourse.bass as bass
import concourse.mybir as mybir

F32 = mybir.dt.float32
F16 = mybir.dt.float16
U16 = mybir.dt.uint16
AF = mybir.ActivationFunctionType
ALU = mybir.AluOpType

D = 2048
H = 16
DK = 128
SCH = 128
NJ = 16
IDIM = 5632
NJJ = 44
GW = 512
DW = 1408

LN_EPS = 1e-5
ATT_EPS = 1e-5
FFN_EPS = 1e-6
SHIFT_EM = [(9, ALU.logical_shift_left), (5, ALU.logical_shift_left),
            (1, ALU.logical_shift_left), (3, ALU.logical_shift_right)]
SIGN_OP = [(0x8, 12), (0x80, 8), (0x800, 4), (0x8000, None)]


def host_prep(inputs, n_cores=8, T=2048):
    r = np.arange(D)
    pi = 512 * (r // 512) + 4 * (r % 128) + ((r // 128) % 4)
    x = np.ascontiguousarray(np.asarray(inputs["L0_attn_x"], dtype=np.float32))
    B, L, _ = x.shape
    segs = n_cores // B
    assert segs * T == L

    jj = np.arange(IDIM)
    gate_perm = 512 * (jj // 512) + 4 * (jj % 128) + ((jj // 128) % 4)
    gu_perm = np.concatenate([gate_perm, IDIM + gate_perm])

    f32 = lambda a: np.asarray(a, dtype=np.float32)
    x_dev = np.ascontiguousarray(x.transpose(0, 2, 1)[:, pi, :])

    shared = {
        "wqT": np.ascontiguousarray(f32(inputs["L0_attn_wq"]).T[pi, :]),
        "wkT": np.ascontiguousarray(f32(inputs["L0_attn_wk"]).T[pi, :]),
        "wvT": np.ascontiguousarray(f32(inputs["L0_attn_wv"]).T[pi, :]),
        "woT": np.ascontiguousarray(f32(inputs["L0_attn_wo"]).T[:, pi]),
        "fq1T": np.ascontiguousarray(f32(inputs["L0_attn_fq1_w"]).T),
        "fq2T": np.ascontiguousarray(f32(inputs["L0_attn_fq2_w"]).T),
        "fk1T": np.ascontiguousarray(f32(inputs["L0_attn_fk1_w"]).T),
        "fk2T": np.ascontiguousarray(f32(inputs["L0_attn_fk2_w"]).T),
        "bq1": f32(inputs["L0_attn_fq1_b"]).reshape(128, 1).copy(),
        "bq2": f32(inputs["L0_attn_fq2_b"]).reshape(128, 1).copy(),
        "bk1": f32(inputs["L0_attn_fk1_b"]).reshape(128, 1).copy(),
        "bk2": f32(inputs["L0_attn_fk2_b"]).reshape(128, 1).copy(),
        "normw": f32(inputs["L0_attn_norm_w"]).reshape(128, 1).copy(),
        "gup": np.ascontiguousarray(
            np.asarray(inputs["L0_ffn_gate_up_packed"]).view(np.uint16)[gu_perm, :].T),
        "gus": np.ascontiguousarray(
            np.repeat(np.asarray(inputs["L0_ffn_gate_up_scales"]), 32, axis=0)[:, gu_perm]),
        "dnp": np.ascontiguousarray(
            np.asarray(inputs["L0_ffn_down_packed"]).view(np.uint16)[pi, :].T),
        "dns": np.ascontiguousarray(
            np.repeat(np.asarray(inputs["L0_ffn_down_scales"]), 32, axis=0)[:, pi]),
        "ident": np.eye(128, dtype=np.float32),
    }
    m = np.zeros((SCH, SCH), np.float32)
    m[0:64, 64:128] = 1.0
    m[0:64, 0:64] = np.triu(np.ones((64, 64), np.float32))
    m[64:128, 64:128] = np.triu(np.ones((64, 64), np.float32))
    shared["mask4"] = np.ascontiguousarray(np.tile(m, (1, 4)))

    per_core = []
    for c in range(n_cores):
        b, seg = c // segs, c % segs
        m8 = np.zeros((128, n_cores), np.float32)
        for rr in range(n_cores):
            if rr // segs == b and rr % segs < seg:
                m8[:, rr] = 1.0
        per_core.append({
            "xs": np.ascontiguousarray(x_dev[b, :, seg * T:(seg + 1) * T]),
            "m8": m8,
        })
    return shared, per_core, pi


class Sem:
    __slots__ = ("h", "n")

    def __init__(self, mk, name):
        self.h = mk.ostack.enter_context(mk.nc.semaphore(name))
        self.n = 0

    def inc(self, instr, k=1):
        instr.then_inc(self.h, k)
        self.n += k
        return (self, self.n)

    def tag(self):
        return (self, self.n)


def wait(eng, tag):
    if tag is not None:
        eng.wait_ge(tag[0].h, tag[1])


class MK:

    def __init__(self, n_cores=8, T=2048, dbg=()):
        self.n_cores = n_cores
        self.dbg = set(dbg)
        self.T = T
        self.NG = T // 512
        nc = bass.Bass("TRN2", target_bir_lowering=False, detect_race_conditions=False)
        self.nc = nc
        self.ostack = ExitStack()

        i = lambda name, shape, dt: nc.dram_tensor(name, shape, dt, kind="ExternalInput").ap()
        self.xs = i("xs", [D, T], F32)
        self.m8 = i("m8", [128, n_cores], F32)
        self.wqT = i("wqT", [D, D], F32)
        self.wkT = i("wkT", [D, D], F32)
        self.wvT = i("wvT", [D, D], F32)
        self.woT = i("woT", [D, D], F32)
        self.fq1T = i("fq1T", [128, 128], F32)
        self.fq2T = i("fq2T", [128, 128], F32)
        self.fk1T = i("fk1T", [128, 128], F32)
        self.fk2T = i("fk2T", [128, 128], F32)
        self.bq1 = i("bq1", [128, 1], F32)
        self.bq2 = i("bq2", [128, 1], F32)
        self.bk1 = i("bk1", [128, 1], F32)
        self.bk2 = i("bk2", [128, 1], F32)
        self.normw = i("normw", [128, 1], F32)
        self.mask4 = i("mask4", [SCH, 4 * SCH], F32)
        self.ident_in = i("ident", [128, 128], F32)
        self.gup = i("gup", [GW, 2 * IDIM], U16)
        self.gus = i("gus", [GW, 2 * IDIM], F16)
        self.dnp = i("dnp", [DW, D], U16)
        self.dns = i("dns", [DW, D], F16)

        self.out = nc.dram_tensor("out", [T, D], F32, kind="ExternalOutput").ap()

        d = lambda name, shape, dt: nc.dram_tensor(
            name, shape, dt,
            kind=("ExternalOutput" if name in self.dbg else "Internal")).ap()
        self.d_wq16 = d("d_wq16", [D, D], F16)
        self.d_wk16 = d("d_wk16", [D, D], F16)
        self.d_wv16 = d("d_wv16", [D, D], F16)
        self.d_wo16 = d("d_wo16", [D, D], F16)
        self.d_kfm = d("d_kfm", [D, T], F16)
        self.d_ktm = d("d_ktm", [T, D], F16)
        self.d_vtm = d("d_vtm", [T, D], F16)
        self.d_xp = d("d_xp", [T, D], F16)
        self.d_xh2 = d("d_xh2", [D, T], F16)
        self.d_xh = d("d_xh", [D, T], F16)
        self.d_h = d("d_h", [IDIM, T], F16)
        self.d_wdn = d("d_wdn", [2 * IDIM, D // 2], F16)
        self.d_contrib = d("d_contrib", [128, D], F32)
        self.d_gath = d("d_gath", [128 * n_cores, D], F32)

        osb = lambda name, shape, dt: self.ostack.enter_context(nc.sbuf_tensor(name, shape, dt))
        self.ones = osb("ones", [128, 128], F16)
        self.ident32 = osb("ident32", [128, 128], F32)
        self.ident16 = osb("ident16", [128, 128], F16)
        self.mask4_sb = osb("mask4_sb", [SCH, 4 * SCH], F32)
        self.m8_sb = osb("m8_sb", [128, n_cores], F32)
        self.consts = osb("consts", [128, 8], F32)
        self.fqk = osb("fqk", [128, 512], F16)
        self.rows = osb("rows", [1, 1536], F32)

        s = lambda name: Sem(self, name)
        self.sem = s
        self.ld = s("ld")
        self.ldg = s("ldg")
        self.stq = s("stq")
        self.sto = s("sto")
        self.pe = s("pe")
        self.dv = s("dv")
        self.ac = s("ac")
        self.cc = s("cc")
        self.collective_done = None

    def build(self):
        nc = self.nc
        with ExitStack() as ph12:
            sb12 = lambda n_, s_, d_: ph12.enter_context(nc.sbuf_tensor(n_, s_, d_))
            self.x16 = [sb12("x16_0", [128, NJ * 512], F16), None]
            self.xh = sb12("xh", [128, NJ * 512], F16)
            self.mrs = sb12("mrs", [1, self.T], F16)
            self.rstd1 = sb12("rstd1", [1, self.T], F16)
            self.S_master = sb12("S_master", [128, D], F32)
            self.S0_16 = sb12("S0_16", [128, D], F16)
            self.gath_sb = sb12("gath_sb", [128, D], F32)
            self.gtmp = sb12("gtmp", [128, D], F32)
            self.S0_32 = sb12("S0_32", [128, D], F32)
            self.startup()
            with ExitStack() as ph1:
                self.phase1(ph1)
            self.barrier()
            self.exchange()
            with ExitStack() as ph2:
                self.phase2(ph2)
            self.barrier()
        with ExitStack() as ph3:
            self.phase3(ph3)
        self.barrier()
        with ExitStack() as ph4:
            self.phase4(ph4)
        self.finale()
        self.ostack.close()
        return nc

    def barrier(self):
        self.nc.all_engine_barrier()

    def psum(self, st, name, shape, dt=F32):
        return st.enter_context(self.nc.psum_tensor(name, shape, dt))

    def sbuf(self, st, name, shape, dt):
        return st.enter_context(self.nc.sbuf_tensor(name, shape, dt))

    def startup(self):
        nc = self.nc
        DV, SY, GP = nc.vector, nc.sync, nc.gpsimd
        DV.memset(self.ones[:, :], 1.0)
        DV.memset(self.S_master[:, :], 0.0)
        self.ld.inc(SY.dma_start(self.ident32[:, :], self.ident_in[:, :]), 16)
        self.ld.inc(SY.dma_start(self.mask4_sb[:, :], self.mask4[:, :]), 16)
        self.ld.inc(SY.dma_start(self.m8_sb[:, :], self.m8[:, :]), 16)
        self.ld.inc(SY.dma_start(self.consts[:, 0:1], self.bq1[:, :]), 16)
        self.ld.inc(SY.dma_start(self.consts[:, 1:2], self.bq2[:, :]), 16)
        self.ld.inc(SY.dma_start(self.consts[:, 2:3], self.bk1[:, :]), 16)
        self.ld.inc(SY.dma_start(self.consts[:, 3:4], self.bk2[:, :]), 16)
        t_c = self.ld.inc(SY.dma_start(self.consts[:, 4:5], self.normw[:, :]), 16)
        self.ldg.inc(GP.dma_start(self.ident16[:, :], self.ident_in[:, :]), 16)
        self.ldg.inc(GP.dma_start(self.fqk[:, 0:128], self.fq1T[:, :]), 16)
        self.ldg.inc(GP.dma_start(self.fqk[:, 128:256], self.fq2T[:, :]), 16)
        self.ldg.inc(GP.dma_start(self.fqk[:, 256:384], self.fk1T[:, :]), 16)
        t_g = self.ldg.inc(GP.dma_start(self.fqk[:, 384:512], self.fk2T[:, :]), 16)
        self.ldg.inc(GP.dma_start(self.d_wk16[:, :], self.wkT[:, :]), 16)
        self.t_wkv = self.ldg.inc(GP.dma_start(self.d_wv16[:, :], self.wvT[:, :]), 16)
        self.ldg.inc(GP.dma_start(self.d_wq16[:, :], self.wqT[:, :]), 16)
        self.t_wcast = self.ldg.inc(GP.dma_start(self.d_wo16[:, :], self.woT[:, :]), 16)
        wait(DV, t_g)
        wait(DV, t_c)
        sc = float(DK ** -0.5)
        DV.tensor_scalar(self.fqk[:, 0:128], self.fqk[:, 0:128], sc, None, ALU.mult)
        self.t_const = self.dv.inc(
            DV.tensor_scalar(self.consts[:, 0:1], self.consts[:, 0:1], sc, None, ALU.mult))

    def ln_rows(self, sums_ap, sqs_ap, eps, rstd_dst16, mrs_dst16, wtags=(), sq_scale=1.0):
        nc = self.nc
        DV, AC = nc.vector, nc.scalar
        rows = self.rows
        for t in wtags:
            wait(DV, t)
        DV.tensor_scalar(rows[0:1, 0:512], sums_ap, 1.0 / D, None, ALU.mult)
        DV.tensor_tensor(rows[0:1, 512:1024], rows[0:1, 0:512], rows[0:1, 0:512], ALU.mult)
        DV.tensor_scalar(rows[0:1, 512:1024], rows[0:1, 512:1024], eps, None, ALU.subtract)
        t_var = self.dv.inc(DV.scalar_tensor_tensor(
            rows[0:1, 1024:1536], sqs_ap, sq_scale / D, rows[0:1, 512:1024],
            ALU.mult, ALU.subtract))
        wait(AC, t_var)
        AC.activation(rows[0:1, 512:1024], rows[0:1, 1024:1536], AF.Ln)
        t_rstd = self.ac.inc(AC.activation(rstd_dst16, rows[0:1, 512:1024], AF.Exp, scale=-0.5))
        wait(DV, t_rstd)
        t_mrs = self.dv.inc(DV.tensor_tensor(mrs_dst16, rows[0:1, 0:512], rstd_dst16, ALU.mult))
        return t_mrs

    def bcast2(self, src_a16, src_b16, ps_a, ps_b, wtag):
        nc = self.nc
        PE = nc.tensor
        wait(PE, wtag)
        PE.matmul(ps_a, self.ones[0:1, :], src_a16, start=True, stop=True)
        return self.pe.inc(PE.matmul(ps_b, self.ones[0:1, :], src_b16, start=True, stop=True))

    def phase1(self, st):
        nc = self.nc
        PE, DV, AC, SY, GP = nc.tensor, nc.vector, nc.scalar, nc.sync, nc.gpsimd
        T = self.T

        self.x16[1] = self.sbuf(st, "x16_1", [128, NJ * 512], F16)
        xsq = self.sbuf(st, "xsq", [128, 512], F16)
        wvb = [self.sbuf(st, f"wvb{k}", [128, NJ * 512], F16) for k in range(2)]
        wkb = [self.sbuf(st, f"wkb{k}", [128, NJ * 128], F16) for k in range(2)]
        kr = self.sbuf(st, "kr", [128, 512], F16)
        k1s = self.sbuf(st, "k1s", [128, 512], F16)
        k2s = self.sbuf(st, "k2s", [128, 512], F16)
        kfmb = self.sbuf(st, "kfmb", [128, 512], F16)
        ktm_asm = self.sbuf(st, "ktm_asm", [128, 4 * 2048], F16)
        vtm_asm = self.sbuf(st, "vtm_asm", [128, 4 * 2048], F16)
        tot_ev = self.sbuf(st, "tot_ev", [128, 2048], F32)
        ps_tot = [self.psum(st, f"ps_tot{i}", [128, 512]) for i in range(4)]
        ps_a = self.psum(st, "ps_a", [128, 512])
        ps_b = self.psum(st, "ps_b", [128, 512])
        ps_c = self.psum(st, "ps_c", [128, 512])
        ps_t16 = self.psum(st, "ps_t16", [128, 512], F16)

        ld_wv = self.sem("ld_wv")
        ld_wk = self.sem("ld_wk")

        def load_wv(ec, buf):
            wait(SY, self.t_wkv)
            return ld_wv.inc(SY.dma_start(
                wvb[buf][:, :].rearrange("p (j o) -> p j o", j=NJ),
                self.d_wv16.rearrange("(j p) o -> p j o", p=128)[:, :, ec * 512:(ec + 1) * 512]), 16)

        def load_wk(h, buf):
            wait(SY, self.t_wkv)
            return ld_wk.inc(SY.dma_start(
                wkb[buf][:, :].rearrange("p (j e) -> p j e", j=NJ),
                self.d_wk16.rearrange("(j p) o -> p j o", p=128)[:, :, h * 128:(h + 1) * 128]), 16)

        t_x = [None, None]
        t_x[0] = self.ldg.inc(GP.dma_start(
            self.x16[0][:, :].rearrange("p (j t) -> p j t", j=NJ),
            self.xs.rearrange("(j p) t -> p j t", p=128)[:, :, 0:512]), 16)
        t_wv = [load_wv(0, 0), None]
        t_wk = [load_wk(0, 0), None]

        t_xh_used = None
        t_xh_spill = None
        t_kfm_spill = None
        t_xb_free = [None, None]
        t_spill_v = None
        t_spill_kt = None
        t_wv_used = [None, None]
        t_wk_used = [None, None]

        for g in range(self.NG):
            t0 = g * 512
            if g + 1 < self.NG:
                wait(GP, t_xb_free[(g + 1) % 2])
                t_x[(g + 1) % 2] = self.ldg.inc(GP.dma_start(
                    self.x16[(g + 1) % 2][:, :].rearrange("p (j t) -> p j t", j=NJ),
                    self.xs.rearrange("(j p) t -> p j t", p=128)[:, :, t0 + 512:t0 + 1024]), 16)
            xb = self.x16[g % 2]

            wait(PE, t_x[g % 2])
            wait(PE, self.t_const)
            if g > 0:
                wait(DV, t_xh_spill)
            for j in range(NJ):
                ip = PE.matmul(ps_a[0:1, :], self.ones[:, 0:1], xb[:, j * 512:(j + 1) * 512],
                               start=(j == 0), stop=(j == NJ - 1))
            t_sx = self.pe.inc(ip)
            wait(AC, t_x[g % 2])
            t_sq_mm = None
            for j in range(NJ):
                wait(AC, t_sq_mm)
                t_sq = self.ac.inc(AC.activation(xsq[:, :], xb[:, j * 512:(j + 1) * 512], AF.Square))
                wait(PE, t_sq)
                t_sq_mm = self.pe.inc(PE.matmul(ps_b[0:1, :], self.ones[:, 0:1], xsq[:, :],
                                                start=(j == 0), stop=(j == NJ - 1)))
            t_mrs = self.ln_rows(ps_a[0:1, :], ps_b[0:1, :], LN_EPS,
                                 self.rstd1[0:1, t0:t0 + 512], self.mrs[0:1, t0:t0 + 512],
                                 wtags=(t_sx, t_sq_mm))
            t_bc = self.bcast2(self.rstd1[0:1, t0:t0 + 512], self.mrs[0:1, t0:t0 + 512],
                               ps_a[:, :], ps_b[:, :], t_mrs)
            wait(DV, t_bc)
            wait(DV, t_xh_used)
            for j in range(NJ):
                DV.tensor_tensor(self.xh[:, j * 512:(j + 1) * 512], xb[:, j * 512:(j + 1) * 512],
                                 ps_a[:, :], ALU.mult)
            for j in range(NJ):
                iv = DV.tensor_tensor(self.xh[:, j * 512:(j + 1) * 512],
                                      self.xh[:, j * 512:(j + 1) * 512], ps_b[:, :], ALU.subtract)
            t_xh = self.dv.inc(iv)
            t_xb_free[g % 2] = t_xh
            wait(SY, t_xh)
            t_xh_spill = self.stq.inc(SY.dma_start(
                self.d_xh.rearrange("(j p) t -> p j t", p=128)[:, :, t0:t0 + 512],
                self.xh[:, :].rearrange("p (j t) -> p j t", j=NJ)), 16)

            wait(PE, t_xh)
            t_v_evac = [None, None]
            for ec in range(4):
                buf = ec % 2
                nec, nbuf = ec + 1, (ec + 1) % 2
                if nec < 4:
                    wait(SY, t_wv_used[nbuf])
                    t_wv[nbuf] = load_wv(nec, nbuf)
                elif g + 1 < self.NG:
                    wait(SY, t_wv_used[nbuf])
                    t_wv[nbuf] = load_wv(0, nbuf)
                wait(PE, t_wv[buf])
                for sub in range(4):
                    pq = ps_a if sub % 2 == 0 else ps_b
                    wait(PE, t_v_evac[sub % 2])
                    for j in range(NJ):
                        ipv = PE.matmul(
                            pq[:, :],
                            self.xh[:, j * 512 + sub * 128:j * 512 + (sub + 1) * 128],
                            wvb[buf][:, j * 512:(j + 1) * 512],
                            start=(j == 0), stop=(j == NJ - 1))
                    t_vmm = self.pe.inc(ipv)
                    wait(DV, t_vmm)
                    t_v_evac[sub % 2] = self.dv.inc(DV.tensor_copy(
                        vtm_asm[:, sub * 2048 + ec * 512:sub * 2048 + (ec + 1) * 512], pq[:, :]))
                t_wv_used[buf] = self.pe.tag()
            wait(SY, t_v_evac[0])
            wait(SY, t_v_evac[1])
            wait(SY, t_spill_v)
            t_spill_v = self.stq.inc(SY.dma_start(
                self.d_vtm.rearrange("(s p) o -> p s o", p=128)[:, 4 * g:4 * (g + 1), :],
                vtm_asm[:, :].rearrange("p (s o) -> p s o", s=4)), 16)

            t_kr_used = None
            t_kmap_used = None
            t_kfm_used = None
            t_ktr_evac = None
            for h in range(H):
                buf = h % 2
                if h + 1 < H:
                    wait(SY, t_wk_used[(h + 1) % 2])
                    t_wk[(h + 1) % 2] = load_wk(h + 1, (h + 1) % 2)
                elif g + 1 < self.NG:
                    wait(SY, t_wk_used[(h + 1) % 2])
                    t_wk[(h + 1) % 2] = load_wk(0, (h + 1) % 2)
                wait(PE, t_kr_used)
                wait(PE, t_wk[buf])
                for j in range(NJ):
                    ik = PE.matmul(ps_c[:, :],
                                   wkb[buf][:, j * 128:(j + 1) * 128],
                                   self.xh[:, j * 512:(j + 1) * 512],
                                   start=(j == 0), stop=(j == NJ - 1))
                t_kraw = self.pe.inc(ik)
                t_wk_used[buf] = t_kraw
                wait(AC, t_kraw)
                t_kr = self.ac.inc(AC.activation(kr[:, :], ps_c[:, :], AF.Copy))
                t_kr_used = t_kr
                wait(PE, t_kr)
                wait(PE, t_kmap_used)
                PE.matmul(ps_a[:, :], self.fqk[:, 256:384], kr[:, :], start=True, stop=True)
                t_kmaps = self.pe.inc(PE.matmul(ps_b[:, :], self.fqk[:, 384:512], kr[:, :],
                                                start=True, stop=True))
                wait(AC, t_kmaps)
                AC.activation(k1s[:, :], ps_a[:, :], AF.Identity, bias=self.consts[:, 2:3])
                t_kbias = self.ac.inc(AC.activation(k2s[:, :], ps_b[:, :], AF.Identity,
                                                    bias=self.consts[:, 3:4]))
                t_kmap_used = t_kbias
                wait(DV, t_kbias)
                wait(DV, t_kfm_used)
                wait(DV, t_kfm_spill)
                t_kfm = self.dv.inc(DV.tensor_tensor(kfmb[:, :], k1s[:, :], k2s[:, :], ALU.mult))
                wait(SY, t_kfm)
                t_kfm_spill = self.stq.inc(SY.dma_start(
                    self.d_kfm[h * 128:(h + 1) * 128, t0:t0 + 512], kfmb[:, :]), 16)
                wait(PE, t_kfm)
                wait(PE, t_ktr_evac)
                for sub in range(4):
                    it = PE.transpose(ps_t16[:, sub * 128:(sub + 1) * 128],
                                      kfmb[:, sub * 128:(sub + 1) * 128], self.ident16[:, :])
                t_ktr = self.pe.inc(it)
                t_kfm_used = t_ktr
                wait(DV, t_ktr)
                wait(DV, t_spill_kt)
                t_ktr_evac = self.dv.inc(DV.tensor_copy(
                    ktm_asm[:, :].rearrange("p (s o) -> p s o", s=4)[:, :, h * 128:(h + 1) * 128],
                    ps_t16[:, :].rearrange("p (s e) -> p s e", s=4)))
            wait(SY, t_ktr_evac)
            t_spill_kt = self.stq.inc(SY.dma_start(
                self.d_ktm.rearrange("(s p) o -> p s o", p=128)[:, 4 * g:4 * (g + 1), :],
                ktm_asm[:, :].rearrange("p (s o) -> p s o", s=4)), 16)

            wait(PE, t_ktr_evac)
            for sub in range(4):
                for h in range(H):
                    itt = PE.matmul(
                        ps_tot[h // 4][:, (h % 4) * 128:(h % 4 + 1) * 128],
                        ktm_asm[:, sub * 2048 + h * 128:sub * 2048 + (h + 1) * 128],
                        vtm_asm[:, sub * 2048 + h * 128:sub * 2048 + (h + 1) * 128],
                        start=(g == 0 and sub == 0 and h % 4 == 0),
                        stop=(g == self.NG - 1 and sub == 3 and h % 4 == 3))
            t_tot = self.pe.inc(itt)
            t_xh_used = t_tot

        wait(DV, t_tot)
        for q in range(4):
            ie = DV.tensor_copy(tot_ev[:, q * 512:(q + 1) * 512], ps_tot[q][:, :])
        t_ev = self.dv.inc(ie)
        wait(SY, t_ev)
        self.stq.inc(SY.dma_start(self.d_contrib[:, :], tot_ev[:, :]), 16)
        self.x16[1] = None

    def exchange(self):
        nc = self.nc
        GP = nc.gpsimd
        gp_sem = self.sem("gp_ex")
        gld = self.sem("gp_gld")
        wait(GP, self.stq.tag())
        t_cc = self.cc.inc(GP.collective_compute(
            "AllGather", ALU.bypass, replica_groups=[list(range(self.n_cores))],
            ins=[self.d_contrib[:, :]], outs=[self.d_gath[:, :]]), 1)
        gsb, acc = self.gath_sb, self.S0_32
        GP.memset(acc[:, :], 0.0)
        wait(GP, t_cc)
        for r in range(self.n_cores):
            t_g = gld.inc(GP.dma_start(gsb[:, :], self.d_gath[128 * r:128 * (r + 1), :]), 16)
            wait(GP, t_g)
            GP.tensor_scalar(self.gtmp[:, :], gsb[:, :], self.m8_sb[:, r:r + 1], None, ALU.mult)
            GP.tensor_tensor(acc[:, :], acc[:, :], self.gtmp[:, :], ALU.add)
        self.collective_done = gp_sem.inc(GP.tensor_copy(self.S0_16[:, :], acc[:, :]))

    def phase2(self, st):
        nc = self.nc
        PE, DV, AC, SY, GP = nc.tensor, nc.vector, nc.scalar, nc.sync, nc.gpsimd
        T = self.T
        sb = lambda n_, s_, d_: self.sbuf(st, n_, s_, d_)

        kfm_r = [sb(f"kfm_r{k}", [128, 4 * 512], F16) for k in range(2)]
        ktm_r = [sb(f"ktm_r{k}", [128, 4 * 512], F16) for k in range(2)]
        vtm_r = [sb(f"vtm_r{k}", [128, 4 * 512], F16) for k in range(2)]
        qraw = sb("qraw", [128, 4 * 512], F16)
        q1s = sb("q1s", [128, 4 * 512], F16)
        q2s = sb("q2s", [128, 4 * 512], F16)
        qfin = sb("qfin", [128, 4 * 512], F16)
        onorm = sb("onorm", [128, H * 512], F16)
        wq_rb = [sb("wq_rb0", [128, NJ * 512], F16)]
        wo_ob = [sb(f"wo_ob{k}", [128, NJ * 128], F16) for k in range(2)]
        S16 = sb("S16", [128, 512], F16)
        sTs = [sb(f"sTs{k}", [SCH, 4 * SCH], F16) for k in range(2)]
        rbc_sb = sb("rbc_sb", [128, 512], F32)
        o32 = self.gtmp
        osq = sb("osq", [128, 512], F16)
        xp = sb("xp", [128, NJ * 512], F16)
        xpt = sb("xpt", [128, 2048], F16)
        xh2b = sb("xh2b", [128, 512], F16)

        ps_o = [self.psum(st, f"ps_o{i}", [128, 512]) for i in range(4)]
        ps_st = self.psum(st, "ps_st", [128, 512])
        ps_x = self.psum(st, "ps_x", [128, 512])
        ps_w = self.psum(st, "ps_w", [128, 512])
        ps_w2 = self.psum(st, "ps_w2", [128, 512])

        ld_kv = self.sem("ld_kv")
        ld_wq = self.sem("ld_wq")
        ld_wo = self.sem("ld_wo")
        ld_x2 = self.sem("ld_x2")

        def load_round(g, r, buf):
            t0 = g * 512
            ld_kv.inc(SY.dma_start(
                kfm_r[buf][:, :].rearrange("p (h t) -> p h t", h=4),
                self.d_kfm.rearrange("(h p) t -> p h t", p=128)[:, 4 * r:4 * (r + 1), t0:t0 + 512]), 16)
            ld_kv.inc(SY.dma_start(
                ktm_r[buf][:, :].rearrange("p (s o) -> p s o", s=4),
                self.d_ktm.rearrange("(s p) o -> p s o", p=128)
                [:, 4 * g:4 * (g + 1), r * 512:(r + 1) * 512]), 16)
            return ld_kv.inc(SY.dma_start(
                vtm_r[buf][:, :].rearrange("p (s o) -> p s o", s=4),
                self.d_vtm.rearrange("(s p) o -> p s o", p=128)
                [:, 4 * g:4 * (g + 1), r * 512:(r + 1) * 512]), 16)

        t_wq_used = [None]

        def load_wq(r, buf):
            wait(SY, self.t_wcast)
            wait(SY, t_wq_used[0])
            return ld_wq.inc(SY.dma_start(
                wq_rb[0][:, :].rearrange("p (j t) -> p j t", j=NJ),
                self.d_wq16.rearrange("(j p) o -> p j o", p=128)[:, :, r * 512:(r + 1) * 512]), 16)

        wait(SY, self.stq.tag())
        xb = self.x16[0]
        ld_xh = self.sem("ld_xh")
        t_x = self.ldg.inc(GP.dma_start(
            xb[:, :].rearrange("p (j t) -> p j t", j=NJ),
            self.xs.rearrange("(j p) t -> p j t", p=128)[:, :, 0:512]), 16)
        t_xh = ld_xh.inc(SY.dma_start(
            self.xh[:, :].rearrange("p (j t) -> p j t", j=NJ),
            self.d_xh.rearrange("(j p) t -> p j t", p=128)[:, :, 0:512]), 16)
        t_kv = [load_round(0, 0, 0), None]
        t_wq = [load_wq(0, 0)]
        t_kv_used = [None, None]

        t_ps_w_free = None
        t_ps_w2_free = None
        t_ps_o_free = [None] * 4
        t_ps_x_free = None
        t_ps_st_free = None
        t_qfin_used = None
        t_onorm_used = None
        t_xh_used = None
        t_sT_used = [None, None]
        t_xb_free = None

        for g in range(self.NG):
            t0 = g * 512
            for r in range(4):
                buf = r % 2
                nbuf = (r + 1) % 2
                if r < 3:
                    wait(SY, t_kv_used[nbuf])
                    t_kv[nbuf] = load_round(g, r + 1, nbuf)
                elif g + 1 < self.NG:
                    wait(SY, t_kv_used[nbuf])
                    t_kv[nbuf] = load_round(g + 1, 0, nbuf)
                wqb = wq_rb[0]
                wait(PE, t_xh)
                wait(DV, t_x)
                wait(PE, t_wq[0])
                for hh in range(4):
                    wait(PE, t_ps_o_free[hh])
                    for j in range(NJ):
                        iq = PE.matmul(ps_o[hh][:, :],
                                       wqb[:, j * 512 + hh * 128:j * 512 + (hh + 1) * 128],
                                       self.xh[:, j * 512:(j + 1) * 512],
                                       start=(j == 0), stop=(j == NJ - 1))
                    tq = self.pe.inc(iq)
                    wait(AC, tq)
                    ta = self.ac.inc(AC.activation(qraw[:, hh * 512:(hh + 1) * 512],
                                                   ps_o[hh][:, :], AF.Copy))
                    t_ps_o_free[hh] = ta
                t_wq_used[0] = self.pe.tag()
                if r < 3:
                    t_wq[0] = load_wq(r + 1, 0)
                elif g + 1 < self.NG:
                    t_wq[0] = load_wq(0, 0)
                for hh in range(4):
                    wait(PE, t_ps_o_free[hh])
                    im = PE.matmul(ps_o[hh][:, :], self.fqk[:, 0:128],
                                   qraw[:, hh * 512:(hh + 1) * 512], start=True, stop=True)
                    tm = self.pe.inc(im)
                    wait(AC, tm)
                    ta = self.ac.inc(AC.activation(q1s[:, hh * 512:(hh + 1) * 512],
                                                   ps_o[hh][:, :], AF.Identity,
                                                   bias=self.consts[:, 0:1]))
                    t_ps_o_free[hh] = ta
                for hh in range(4):
                    wait(PE, t_ps_o_free[hh])
                    im = PE.matmul(ps_o[hh][:, :], self.fqk[:, 128:256],
                                   qraw[:, hh * 512:(hh + 1) * 512], start=True, stop=True)
                    tm = self.pe.inc(im)
                    wait(AC, tm)
                    ta = self.ac.inc(AC.activation(q2s[:, hh * 512:(hh + 1) * 512],
                                                   ps_o[hh][:, :], AF.Identity,
                                                   bias=self.consts[:, 1:2]))
                    t_ps_o_free[hh] = ta
                wait(DV, self.ac.tag())
                wait(DV, t_qfin_used)
                for hh in range(4):
                    iv = DV.tensor_tensor(qfin[:, hh * 512:(hh + 1) * 512],
                                          q1s[:, hh * 512:(hh + 1) * 512],
                                          q2s[:, hh * 512:(hh + 1) * 512], ALU.mult)
                t_qfin = self.dv.inc(iv)

                t_update = None
                t_madd = None
                wait(PE, t_kv[buf])
                wait(PE, t_qfin)
                for scn in range(4):
                    wait(DV, t_madd)
                    t_scopy = self.dv.inc(DV.tensor_copy(
                        S16[:, :], self.S_master[:, r * 512:(r + 1) * 512]))
                    wait(PE, t_sT_used[scn % 2])
                    wait(PE, t_ps_x_free)
                    for hh in range(4):
                        isT = PE.matmul(
                            ps_x[0:SCH, hh * SCH:(hh + 1) * SCH],
                            kfm_r[buf][:, hh * 512 + scn * 128:hh * 512 + (scn + 1) * 128],
                            qfin[:, hh * 512 + scn * 128:hh * 512 + (scn + 1) * 128],
                            start=True, stop=True)
                    t_sT = self.pe.inc(isT)
                    wait(DV, t_sT)
                    t_sTe = self.dv.inc(DV.tensor_tensor(sTs[scn % 2][:, :], ps_x[0:SCH, :],
                                                         self.mask4_sb[:, :], ALU.mult))
                    t_ps_x_free = t_sTe
                    wait(PE, t_scopy)
                    wait(PE, t_sTe)
                    for hh in range(4):
                        PE.matmul(ps_o[hh][:, scn * 128:(scn + 1) * 128],
                                  S16[:, hh * 128:(hh + 1) * 128],
                                  qfin[:, hh * 512 + scn * 128:hh * 512 + (scn + 1) * 128],
                                  start=(scn == 0), stop=False)
                        PE.matmul(ps_o[hh][:, scn * 128:(scn + 1) * 128],
                                  vtm_r[buf][:, scn * 512 + hh * 128:scn * 512 + (hh + 1) * 128],
                                  sTs[scn % 2][:, hh * SCH:(hh + 1) * SCH],
                                  start=False, stop=False)
                    wait(PE, t_ps_st_free)
                    for hh in range(4):
                        iu = PE.matmul(ps_st[:, hh * 128:(hh + 1) * 128],
                                       ktm_r[buf][:, scn * 512 + hh * 128:scn * 512 + (hh + 1) * 128],
                                       vtm_r[buf][:, scn * 512 + hh * 128:scn * 512 + (hh + 1) * 128],
                                       start=(hh == 0), stop=(hh == 3))
                    t_update = self.pe.inc(iu)
                    t_sT_used[scn % 2] = t_update
                    wait(DV, t_update)
                    t_madd = self.dv.inc(DV.tensor_tensor(
                        self.S_master[:, r * 512:(r + 1) * 512], ps_st[:, :],
                        self.S_master[:, r * 512:(r + 1) * 512], ALU.add))
                    t_ps_st_free = t_madd
                t_kv_used[buf] = t_update

                wait(PE, self.collective_done)
                wait(PE, t_ps_st_free)
                wait(PE, t_ps_w_free)
                stat_slots = [(ps_st, 0), (ps_st, 64), (ps_w, 0), (ps_w, 64)]
                t_corrs = []
                for hh in range(4):
                    h = 4 * r + hh
                    ic = PE.matmul(ps_o[hh][:, :], self.S0_16[:, h * 128:(h + 1) * 128],
                                   qfin[:, hh * 512:(hh + 1) * 512], start=False, stop=True)
                    t_corrs.append(self.pe.inc(ic))
                t_qfin_used = self.pe.tag()
                t_o32 = []
                for hh in range(4):
                    wait(DV, t_corrs[hh])
                    tcp = self.dv.inc(DV.tensor_copy(o32[:, hh * 512:(hh + 1) * 512],
                                                     ps_o[hh][:, :]))
                    t_o32.append(tcp)
                    t_ps_o_free[hh] = tcp
                t_stats = []
                t_prev_stat = None
                for hh in range(4):
                    wait(AC, t_o32[hh])
                    wait(AC, t_prev_stat)
                    t_osq = self.ac.inc(AC.activation(osq[:, :], o32[:, hh * 512:(hh + 1) * 512],
                                                      AF.Square, scale=0.25))
                    wait(PE, t_osq)
                    bank, row = stat_slots[hh]
                    t_stat = self.pe.inc(PE.matmul(bank[row:row + 1, :], self.ones[:, 0:1],
                                                   osq[:, :], start=True, stop=True))
                    t_prev_stat = t_stat
                    t_stats.append((bank, row, t_stat))
                t_rbc_used = None
                t_last_row_read = None
                for hh in range(4):
                    h = 4 * r + hh
                    bank, row, t_stat = t_stats[hh]
                    wait(DV, t_stat)
                    t_ms = self.dv.inc(DV.tensor_scalar(
                        self.rows[0:1, 0:512], bank[row:row + 1, :],
                        16.0 / DK, ATT_EPS, ALU.mult, ALU.add))
                    t_last_row_read = t_ms
                    wait(AC, t_ms)
                    AC.activation(self.rows[0:1, 512:1024], self.rows[0:1, 0:512], AF.Ln)
                    t_rr = self.ac.inc(AC.activation(
                        self.rows[0:1, 1024:1280].bitcast(F16)[0:1, 0:512],
                        self.rows[0:1, 512:1024], AF.Exp, scale=-0.5))
                    wait(PE, t_rr)
                    wait(PE, t_ps_x_free)
                    t_rbc = self.pe.inc(PE.matmul(
                        ps_x[:, :], self.ones[0:1, :],
                        self.rows[0:1, 1024:1280].bitcast(F16)[0:1, 0:512],
                        start=True, stop=True))
                    wait(DV, t_rbc)
                    wait(DV, t_rbc_used)
                    t_rc = self.dv.inc(DV.tensor_copy(rbc_sb[:, :], ps_x[:, :]))
                    t_ps_x_free = t_rc
                    wait(DV, t_onorm_used)
                    t_on = self.dv.inc(DV.scalar_tensor_tensor(
                        onorm[:, h * 512:(h + 1) * 512], o32[:, hh * 512:(hh + 1) * 512],
                        self.consts[:, 4:5], rbc_sb[:, :], ALU.mult, ALU.mult))
                    t_rbc_used = t_on
                t_ps_st_free = t_last_row_read
                t_ps_w_free = t_last_row_read
            t_onorm = self.dv.tag()

            if g + 1 < self.NG:
                wait(SY, t_qfin_used)
                t_xh = ld_xh.inc(SY.dma_start(
                    self.xh[:, :].rearrange("p (j t) -> p j t", j=NJ),
                    self.d_xh.rearrange("(j p) t -> p j t", p=128)
                    [:, :, t0 + 512:t0 + 1024]), 16)

            t_wo = [None, None]
            wait(SY, self.t_wcast)
            t_wo[0] = ld_wo.inc(SY.dma_start(
                wo_ob[0][:, :].rearrange("p (h o) -> p h o", h=NJ),
                self.d_wo16.rearrange("(h p) o -> p h o", p=128)[:, :, 0:128]), 16)
            t_xp_evac = None
            t_c16mm = None
            t_sq2 = None
            t_wo_used = [None, None]
            ps_wo = [ps_w2, ps_w]
            t_ps_wo_free = [t_ps_w2_free, t_ps_w_free]
            for ot in range(NJ):
                if ot + 1 < NJ:
                    wait(SY, t_wo_used[(ot + 1) % 2])
                    t_wo[(ot + 1) % 2] = ld_wo.inc(SY.dma_start(
                        wo_ob[(ot + 1) % 2][:, :].rearrange("p (h o) -> p h o", h=NJ),
                        self.d_wo16.rearrange("(h p) o -> p h o", p=128)
                        [:, :, (ot + 1) * 128:(ot + 2) * 128]), 16)
                wait(PE, t_onorm)
                wait(PE, t_wo[ot % 2])
                wait(PE, t_ps_wo_free[ot % 2])
                pw = ps_wo[ot % 2]
                for h in range(H):
                    iw = PE.matmul(pw[:, :], wo_ob[ot % 2][:, h * 128:(h + 1) * 128],
                                   onorm[:, h * 512:(h + 1) * 512],
                                   start=(h == 0), stop=(h == H - 1))
                t_womm = self.pe.inc(iw)
                t_wo_used[ot % 2] = t_womm
                wait(DV, t_womm)
                t_xp_evac = self.dv.inc(DV.tensor_tensor(
                    xp[:, ot * 512:(ot + 1) * 512], pw[:, :],
                    xb[:, ot * 512:(ot + 1) * 512], ALU.add))
                t_ps_wo_free[ot % 2] = t_xp_evac
                wait(PE, t_xp_evac)
                t_c16mm = self.pe.inc(PE.matmul(ps_o[0][0:1, 0:512], self.ones[:, 0:1],
                                                xp[:, ot * 512:(ot + 1) * 512],
                                                start=(ot == 0), stop=(ot == NJ - 1)))
                wait(AC, t_xp_evac)
                wait(AC, t_sq2)
                t_sqa = self.ac.inc(AC.activation(osq[:, :], xp[:, ot * 512:(ot + 1) * 512],
                                                  AF.Square, scale=0.5))
                wait(PE, t_sqa)
                t_sq2 = self.pe.inc(PE.matmul(ps_o[1][0:1, 0:512], self.ones[:, 0:1], osq[:, :],
                                              start=(ot == 0), stop=(ot == NJ - 1)))
            t_ps_w2_free = t_ps_wo_free[0]
            t_ps_w_free = t_ps_wo_free[1]
            t_xb_free = t_xp_evac
            t_mrs2 = self.ln_rows(ps_o[0][0:1, 0:512], ps_o[1][0:1, 0:512], LN_EPS,
                                  self.rows[0:1, 1024:1280].bitcast(F16)[0:1, 0:512],
                                  self.rows[0:1, 1280:1536].bitcast(F16)[0:1, 0:512],
                                  wtags=(t_c16mm, t_sq2), sq_scale=4.0)
            t_bc2 = self.bcast2(self.rows[0:1, 1024:1280].bitcast(F16)[0:1, 0:512],
                                self.rows[0:1, 1280:1536].bitcast(F16)[0:1, 0:512],
                                ps_o[0][:, :], ps_o[1][:, :], t_mrs2)
            t_xh2_st = None
            t_xh2_dv = None
            for ot in range(NJ):
                wait(DV, t_bc2)
                wait(DV, t_xh2_st)
                DV.tensor_tensor(xh2b[:, :], xp[:, ot * 512:(ot + 1) * 512], ps_o[0][:, :],
                                 ALU.mult)
                t_xh2_dv = self.dv.inc(DV.tensor_tensor(xh2b[:, :], xh2b[:, :], ps_o[1][:, :],
                                                        ALU.subtract))
                wait(SY, t_xh2_dv)
                t_xh2_st = self.stq.inc(SY.dma_start(
                    self.d_xh2[ot * 128:(ot + 1) * 128, t0:t0 + 512], xh2b[:, :]), 16)
            t_ps_o_free[0] = t_xh2_dv
            t_ps_o_free[1] = t_xh2_dv
            if g + 1 < self.NG:
                wait(GP, t_xb_free)
                t_x = self.ldg.inc(GP.dma_start(
                    xb[:, :].rearrange("p (j t) -> p j t", j=NJ),
                    self.xs.rearrange("(j p) t -> p j t", p=128)
                    [:, :, t0 + 512:t0 + 1024]), 16)
            t_tr_evac = None
            t_xpt_st = None
            ps_x16 = ps_x[:, 0:256].bitcast(F16)
            for tc in range(4):
                for j4 in range(4):
                    wait(PE, t_tr_evac)
                    wait(PE, t_ps_x_free)
                    for jj in range(4):
                        j = j4 * 4 + jj
                        itr = PE.transpose(ps_x16[:, jj * 128:(jj + 1) * 128],
                                           xp[:, j * 512 + tc * 128:j * 512 + (tc + 1) * 128],
                                           self.ident16[:, :])
                    t_tr = self.pe.inc(itr)
                    wait(DV, t_tr)
                    wait(DV, t_xpt_st)
                    t_tr_evac = self.dv.inc(DV.tensor_copy(
                        xpt[:, j4 * 512:(j4 + 1) * 512], ps_x16[:, 0:512]))
                    t_ps_x_free = t_tr_evac
                wait(SY, t_tr_evac)
                t_xpt_st = self.stq.inc(SY.dma_start(
                    self.d_xp[t0 + tc * 128:t0 + (tc + 1) * 128, :], xpt[:, :]), 16)
            t_xh_used = t_qfin_used
            t_onorm_used = self.pe.tag()

    def phase3(self, st):
        nc = self.nc
        PE, DV, AC, SY, GP = nc.tensor, nc.vector, nc.scalar, nc.sync, nc.gpsimd
        T = self.T
        sb = lambda n_, s_, d_: self.sbuf(st, n_, s_, d_)

        xh2 = sb("xh2_full", [128, NJ * T], F16)
        wait(SY, self.stq.tag())
        t_xh2 = self.ld.inc(SY.dma_start(
            xh2[:, :].rearrange("p (j t) -> p j t", j=NJ),
            self.d_xh2.rearrange("(j p) t -> p j t", p=128)), 16)
        gpk = [sb(f"gpk{k}", [128, 4 * 256], U16) for k in range(2)]
        gsc = [sb(f"gsc{k}", [128, 4 * 256], F16) for k in range(2)]
        wtil = [sb(f"wtil{k}", [128, 16 * 256], F16) for k in range(2)]
        hsil = sb("hsil", [128, 1024], F16)
        up16 = sb("up16", [128, 1024], F16)
        hbuf = sb("hbuf", [128, T], F16)
        nbuf = sb("nbuf", [128, 512], U16)
        dpk = [sb(f"dpk{k}", [128, 1024], U16) for k in range(2)]
        dsc = [sb(f"dsc{k}", [128, 1024], F16) for k in range(2)]
        dslab = [sb(f"dslab{k}", [128, 1024], F16) for k in range(2)]
        dnb = sb("dnb", [128, 2048], U16)
        ps_g = self.psum(st, "ps_g", [128, 1024])
        ps_u = self.psum(st, "ps_u", [128, 1024])
        ps_g2 = self.psum(st, "ps_g2", [128, 1024])
        ps_u2 = self.psum(st, "ps_u2", [128, 1024])

        ld_gu = self.sem("ld_gu")
        ld_dn = self.sem("ld_dn")

        def load_pair(jjp, buf):
            t = None
            for w in range(4):
                ld_gu.inc(SY.dma_start(gpk[buf][:, w * 256:w * 256 + 128],
                          self.gup[w * 128:(w + 1) * 128, jjp * 128:(jjp + 1) * 128]), 16)
                ld_gu.inc(SY.dma_start(gpk[buf][:, w * 256 + 128:(w + 1) * 256],
                          self.gup[w * 128:(w + 1) * 128,
                                   IDIM + jjp * 128:IDIM + (jjp + 1) * 128]), 16)
                ld_gu.inc(SY.dma_start(gsc[buf][:, w * 256:w * 256 + 128],
                          self.gus[w * 128:(w + 1) * 128, jjp * 128:(jjp + 1) * 128]), 16)
                t = ld_gu.inc(SY.dma_start(
                    gsc[buf][:, w * 256 + 128:(w + 1) * 256],
                    self.gus[w * 128:(w + 1) * 128,
                             IDIM + jjp * 128:IDIM + (jjp + 1) * 128]), 16)
            return t

        def dq_ops(dst_ap, pk_ap, sc_ap, width, scratch):
            last = None
            for nb in range(4):
                amt, op = SHIFT_EM[nb]
                DV.tensor_scalar(scratch[:, 0:width], pk_ap, amt, 0x0E00, op, ALU.bitwise_and)
                sm, sa = SIGN_OP[nb]
                if sa is None:
                    DV.tensor_scalar(scratch[:, width:2 * width], pk_ap, sm, None,
                                     ALU.bitwise_and)
                else:
                    DV.tensor_scalar(scratch[:, width:2 * width], pk_ap, sm, sa,
                                     ALU.bitwise_and, ALU.logical_shift_left)
                DV.tensor_tensor(scratch[:, 0:width], scratch[:, 0:width],
                                 scratch[:, width:2 * width], ALU.bitwise_or)
                last = DV.scalar_tensor_tensor(
                    dst_ap[:, nb * width:(nb + 1) * width],
                    scratch[:, 0:width].bitcast(F16), 32768.0, sc_ap,
                    ALU.mult, ALU.mult)
            return last

        t_pair = [None, None]
        t_pair[0] = load_pair(0, 0)
        t_wdq_used = [None, None]
        t_h_done = None
        t_hb_used = None
        t_dn_ld = [None, None]
        t_dn_st = [None, None]
        dn_units = [(half, w2) for half in range(2) for w2 in range(11)]
        dn_i = 0

        def load_dn_unit(i):
            if i >= len(dn_units):
                return
            half, w2 = dn_units[i]
            par = i % 2
            wait(SY, t_dn_st[0])
            wait(SY, t_dn_st[1])
            ld_dn.inc(SY.dma_start(dpk[par][:, :], self.dnp[w2 * 128:(w2 + 1) * 128,
                                   half * 1024:(half + 1) * 1024]), 16)
            t_dn_ld[par] = ld_dn.inc(SY.dma_start(
                dsc[par][:, :], self.dns[w2 * 128:(w2 + 1) * 128,
                                         half * 1024:(half + 1) * 1024]), 16)

        def emit_dn_unit():
            nonlocal dn_i
            if dn_i >= len(dn_units):
                return
            half, w2 = dn_units[dn_i]
            par = dn_i % 2
            dn_i += 1
            load_dn_unit(dn_i)
            for nb in range(4):
                amt, op = SHIFT_EM[nb]
                wait(DV, t_dn_ld[par])
                wait(DV, t_dn_st[nb % 2])
                DV.tensor_scalar(dnb[:, 0:1024], dpk[par][:, :], amt, 0x0E00, op,
                                 ALU.bitwise_and)
                sm, sa = SIGN_OP[nb]
                if sa is None:
                    DV.tensor_scalar(dnb[:, 1024:2048], dpk[par][:, :], sm, None,
                                     ALU.bitwise_and)
                else:
                    DV.tensor_scalar(dnb[:, 1024:2048], dpk[par][:, :], sm, sa,
                                     ALU.bitwise_and, ALU.logical_shift_left)
                DV.tensor_tensor(dnb[:, 0:1024], dnb[:, 0:1024], dnb[:, 1024:2048],
                                 ALU.bitwise_or)
                t_sl = self.dv.inc(DV.scalar_tensor_tensor(
                    dslab[nb % 2][:, :], dnb[:, 0:1024].bitcast(F16), 32768.0,
                    dsc[par][:, :], ALU.mult, ALU.mult))
                wait(SY, t_sl)
                t_dn_st[nb % 2] = self.stq.inc(SY.dma_start(
                    self.d_wdn[half * IDIM + (4 * w2 + nb) * 128:
                               half * IDIM + (4 * w2 + nb + 1) * 128, :],
                    dslab[nb % 2][:, :]), 16)

        NB2 = max(1, T // 1024)
        BW = T // NB2

        def dequant_pair(jjp):
            buf = jjp % 2
            wait(DV, t_pair[buf])
            wait(DV, t_wdq_used[buf])
            idq = None
            for w in range(4):
                idq = dq_ops(wtil[buf][:, w * 1024:(w + 1) * 1024],
                             gpk[buf][:, w * 256:(w + 1) * 256],
                             gsc[buf][:, w * 256:(w + 1) * 256], 256, nbuf)
            return self.dv.inc(idq)

        t_dq = [dequant_pair(0), None]
        load_dn_unit(0)
        for jjp in range(NJJ):
            buf = jjp % 2
            if jjp + 1 < NJJ:
                t_pair[(jjp + 1) % 2] = load_pair(jjp + 1, (jjp + 1) % 2)
                t_dq[(jjp + 1) % 2] = dequant_pair(jjp + 1)
            wait(PE, t_dq[buf])
            wait(PE, t_xh2)
            for tb2 in range(NB2):
                pg = ps_g if tb2 % 2 == 0 else ps_g2
                pu = ps_u if tb2 % 2 == 0 else ps_u2
                wait(PE, t_h_done)
                ig = None
                for w in range(4):
                    for nb in range(4):
                        j = 4 * w + nb
                        base = (w * 4 + nb) * 256
                        for q in range(BW // 512):
                            PE.matmul(pg[:, q * 512:(q + 1) * 512],
                                      wtil[buf][:, base:base + 128],
                                      xh2[:, j * T + tb2 * BW + q * 512:
                                          j * T + tb2 * BW + (q + 1) * 512],
                                      start=(w == 0 and nb == 0), stop=(w == 3 and nb == 3))
                            ig = PE.matmul(pu[:, q * 512:(q + 1) * 512],
                                           wtil[buf][:, base + 128:base + 256],
                                           xh2[:, j * T + tb2 * BW + q * 512:
                                               j * T + tb2 * BW + (q + 1) * 512],
                                           start=(w == 0 and nb == 0), stop=(w == 3 and nb == 3))
                t_gu = self.pe.inc(ig)
                if tb2 == NB2 - 1:
                    t_wdq_used[buf] = t_gu
                wait(AC, t_gu)
                AC.activation(hsil[:, 0:BW], pg[:, 0:BW], AF.Silu, scale=0.5)
                t_silu = self.ac.inc(AC.activation(up16[:, 0:BW], pu[:, 0:BW], AF.Copy,
                                                   scale=float(0.5 * 2 ** -7)))
                wait(DV, t_silu)
                wait(DV, t_hb_used)
                t_h_done = self.dv.inc(DV.tensor_tensor(hbuf[:, tb2 * BW:(tb2 + 1) * BW],
                                                        hsil[:, 0:BW], up16[:, 0:BW], ALU.mult))
            wait(SY, t_h_done)
            t_hb_used = self.stq.inc(SY.dma_start(self.d_h[jjp * 128:(jjp + 1) * 128, :],
                                                  hbuf[:, :]), 16)
            if jjp % 2 == 1:
                emit_dn_unit()
        while dn_i < len(dn_units):
            emit_dn_unit()

    def phase4(self, st):
        nc = self.nc
        PE, DV, AC, SY, GP = nc.tensor, nc.vector, nc.scalar, nc.sync, nc.gpsimd
        T = self.T
        NT = T // 128
        sb = lambda n_, s_, d_: self.sbuf(st, n_, s_, d_)
        slab = sb("slab", [128, NJJ * 1024], F16)
        hbuf2 = [sb(f"hb2_{k}", [128, NJJ * 128], F16) for k in range(2)]
        xpb = [sb(f"xpb{k}", [128, D], F16) for k in range(2)]
        y0 = sb("y0", [128, NT * 1024], F16)
        ysum = sb("ysum", [128, NT], F32)
        outb = [sb(f"outb{k}", [128, D], F32) for k in range(2)]
        sq2 = sb("sq2", [128, 1024], F32)
        rst = sb("rst", [128, 4], F32)
        ps_y = [self.psum(st, f"ps_y{i}", [128, 512]) for i in range(4)]

        ld_sl = self.sem("ld_sl")
        ld_h2 = self.sem("ld_h2")
        ld_xp2 = self.sem("ld_xp2")

        def load_h(tc, buf):
            return ld_h2.inc(SY.dma_start(
                hbuf2[buf][:, :].rearrange("p (jj t) -> p jj t", jj=NJJ),
                self.d_h.rearrange("(jj p) t -> p jj t", p=128)[:, :, tc * 128:(tc + 1) * 128]), 16)

        def load_xp(tc, buf):
            return ld_xp2.inc(SY.dma_start(xpb[buf][:, :],
                                           self.d_xp[tc * 128:(tc + 1) * 128, :]), 16)

        t_out_st = [None, None]
        t_sq_done = [None, None]
        t_h2_used = [None, None]
        t_xp_used = [None, None]
        for half in range(2):
            wait(SY, self.stq.tag())
            if half == 1:
                wait(SY, self.pe.tag())
            t_slab = ld_sl.inc(SY.dma_start(
                slab[:, :].rearrange("p (jj o) -> p jj o", jj=NJJ),
                self.d_wdn[half * IDIM:(half + 1) * IDIM, :]
                .rearrange("(jj p) o -> p jj o", p=128)), 16)
            t_h2 = [None, None]
            t_xpl = [None, None]
            t_h2[0] = load_h(0, 0)
            if half == 1:
                t_xpl[0] = load_xp(0, 0)
            t_y_evac = [None, None]
            for tc in range(NT):
                buf = tc % 2
                nb2 = (tc + 1) % 2
                if tc + 1 < NT:
                    wait(SY, t_h2_used[nb2])
                    t_h2[nb2] = load_h(tc + 1, nb2)
                    if half == 1:
                        wait(SY, t_xp_used[nb2])
                        t_xpl[nb2] = load_xp(tc + 1, nb2)
                wait(PE, t_slab)
                wait(PE, t_h2[buf])
                iy = None
                for sub in range(2):
                    pq = ps_y[2 * (tc % 2) + sub]
                    wait(PE, t_y_evac[tc % 2])
                    for jj in range(NJJ):
                        iy = PE.matmul(pq[:, :], hbuf2[buf][:, jj * 128:(jj + 1) * 128],
                                       slab[:, jj * 1024 + sub * 512:jj * 1024 + (sub + 1) * 512],
                                       start=(jj == 0), stop=(jj == NJJ - 1))
                t_ymm = self.pe.inc(iy)
                t_h2_used[buf] = t_ymm
                if half == 0:
                    wait(DV, t_ymm)
                    DV.tensor_scalar(y0[:, tc * 1024:tc * 1024 + 512],
                                     ps_y[2 * (tc % 2)][:, :], 0.25, None, ALU.mult)
                    t_ye = self.dv.inc(DV.tensor_scalar(
                        y0[:, tc * 1024 + 512:(tc + 1) * 1024],
                        ps_y[2 * (tc % 2) + 1][:, :], 0.25, None, ALU.mult))
                    t_y_evac[tc % 2] = t_ye
                    wait(AC, t_ye)
                    wait(AC, t_sq_done[tc % 2])
                    t_sq_done[tc % 2] = self.ac.inc(AC.activation(
                        sq2[:, :], y0[:, tc * 1024:(tc + 1) * 1024], AF.Square,
                        accum_out=ysum[:, tc:tc + 1]))
                else:
                    wait(AC, t_ymm)
                    wait(AC, t_y_evac[tc % 2])
                    wait(AC, t_y_evac[(tc + 1) % 2])
                    AC.activation(sq2[:, 0:512], ps_y[2 * (tc % 2)][:, :], AF.Square,
                                  scale=0.25, accum_out=rst[:, 0:1])
                    t_sqa = self.ac.inc(AC.activation(
                        sq2[:, 512:1024], ps_y[2 * (tc % 2) + 1][:, :], AF.Square,
                        scale=0.25, accum_out=rst[:, 1:2]))
                    t_sq_done[tc % 2] = t_sqa
                    wait(DV, t_sqa)
                    ta1 = self.dv.inc(DV.tensor_tensor(rst[:, 0:1], rst[:, 0:1],
                                                       rst[:, 1:2], ALU.add))
                    wait(DV, ta1)
                    ta2 = self.dv.inc(DV.tensor_tensor(rst[:, 0:1], rst[:, 0:1],
                                                       ysum[:, tc:tc + 1], ALU.add))
                    wait(DV, ta2)
                    t_vr = self.dv.inc(DV.tensor_scalar(rst[:, 0:1], rst[:, 0:1], 65536.0 / D,
                                                        FFN_EPS, ALU.mult, ALU.add))
                    wait(AC, t_vr)
                    tl1 = self.ac.inc(AC.activation(rst[:, 1:2], rst[:, 0:1], AF.Ln))
                    wait(AC, tl1)
                    t_rs = self.ac.inc(AC.activation(rst[:, 2:3], rst[:, 1:2], AF.Exp,
                                                     scale=-0.5))
                    wait(DV, t_rs)
                    DV.tensor_scalar(rst[:, 3:4], rst[:, 2:3], 64.0, None, ALU.mult)
                    ts2 = self.dv.inc(DV.tensor_scalar(rst[:, 1:2], rst[:, 2:3], 256.0,
                                                       None, ALU.mult))
                    wait(DV, ts2)
                    wait(DV, t_xpl[buf])
                    wait(DV, t_out_st[tc % 2])
                    ob = outb[tc % 2]
                    iv = None
                    for w in range(2):
                        for nb in range(4):
                            fb = 4 * w + nb
                            DV.scalar_tensor_tensor(
                                ob[:, 512 * w + nb:512 * (w + 1):4],
                                y0[:, tc * 1024 + fb * 128:tc * 1024 + (fb + 1) * 128],
                                rst[:, 1:2],
                                xpb[buf][:, fb * 128:(fb + 1) * 128],
                                ALU.mult, ALU.add)
                    for w in range(2):
                        for nb in range(4):
                            fb = 4 * w + nb
                            src = ps_y[2 * (tc % 2) + fb // 4]
                            iv = DV.scalar_tensor_tensor(
                                ob[:, 1024 + 512 * w + nb:1024 + 512 * (w + 1):4],
                                src[:, (fb % 4) * 128:(fb % 4 + 1) * 128],
                                rst[:, 3:4],
                                xpb[buf][:, 1024 + fb * 128:1024 + (fb + 1) * 128],
                                ALU.mult, ALU.add)
                    t_ob = self.dv.inc(iv)
                    t_y_evac[tc % 2] = t_ob
                    t_xp_used[buf] = t_ob
                    wait(SY, t_ob)
                    t_out_st[tc % 2] = self.sto.inc(
                        SY.dma_start(self.out[tc * 128:(tc + 1) * 128, :], ob[:, :]), 16)
        if "d_ysum" in self.dbg:
            dy = self.nc.dram_tensor("d_ysum", [128, NT], F32, kind="ExternalOutput").ap()
            wait(SY, self.dv.tag())
            self.stq.inc(SY.dma_start(dy[:, :], ysum[:, :]), 16)
        if "d_rst" in self.dbg:
            drs = self.nc.dram_tensor("d_rst", [128, 4], F32, kind="ExternalOutput").ap()
            wait(SY, self.dv.tag())
            self.stq.inc(SY.dma_start(drs[:, :], rst[:, :]), 16)
        if "d_y0" in self.dbg:
            dy0 = self.nc.dram_tensor("d_y0", [128, NT * 1024], F32, kind="ExternalOutput").ap()
            wait(SY, self.dv.tag())
            nc.vector.tensor_copy(sq2[:, :], y0[:, 0:1024])
            self.dv.inc(nc.vector.tensor_copy(sq2[:, :], y0[:, 0:1024]))
            for tcq in range(NT):
                iw = nc.vector.tensor_copy(sq2[:, :], y0[:, tcq * 1024:(tcq + 1) * 1024])
                tq = self.dv.inc(iw)
                wait(SY, tq)
                self.stq.inc(SY.dma_start(dy0[:, tcq * 1024:(tcq + 1) * 1024], sq2[:, :]), 16)

    def finale(self):
        nc = self.nc
        SY = nc.sync
        wait(SY, self.sto.tag())
        wait(SY, self.stq.tag())
        wait(SY, self.ld.tag())
        wait(nc.gpsimd, self.ldg.tag())


def build(n_cores=8, T=2048, dbg=()):
    mk = MK(n_cores=n_cores, T=T, dbg=dbg)
    nc = mk.build()
    return nc, mk


def kernel(**inputs):
    from concourse.bass_utils import run_bass_kernel_spmd
    n_cores, T = 8, 2048
    shared, per_core, pi = host_prep(inputs, n_cores, T)
    nc, _ = build(n_cores, T)
    in_maps = [{**shared, **pc} for pc in per_core]
    res = run_bass_kernel_spmd(nc, in_maps, core_ids=list(range(n_cores)))
    B, L = 2, 8192
    segs = n_cores // B
    out = np.empty((B, L, D), np.float32)
    for c in range(n_cores):
        b, seg = c // segs, c % segs
        out[b, seg * T:(seg + 1) * T, :] = res.results[c]["out"]
    return out


# version 04 (final)
# speedup vs baseline: 1.1208x; 1.1208x over previous
import numpy as np
from contextlib import ExitStack

import concourse.bass as bass
import concourse.mybir as mybir

F32 = mybir.dt.float32
F16 = mybir.dt.float16
U16 = mybir.dt.uint16
AF = mybir.ActivationFunctionType
ALU = mybir.AluOpType

D = 2048
H = 16
DK = 128
SCH = 128
NJ = 16
IDIM = 5632
NJJ = 44
GW = 512
DW = 1408

LN_EPS = 1e-5
ATT_EPS = 1e-5
FFN_EPS = 1e-6
SHIFT_EM = [(9, ALU.logical_shift_left), (5, ALU.logical_shift_left),
            (1, ALU.logical_shift_left), (3, ALU.logical_shift_right)]
SIGN_OP = [(0x8, 12), (0x80, 8), (0x800, 4), (0x8000, None)]


def host_prep(inputs, n_cores=8, T=2048):
    r = np.arange(D)
    pi = 512 * (r // 512) + 4 * (r % 128) + ((r // 128) % 4)
    x = np.ascontiguousarray(np.asarray(inputs["L0_attn_x"], dtype=np.float32))
    B, L, _ = x.shape
    segs = n_cores // B
    assert segs * T == L

    jj = np.arange(IDIM)
    gate_perm = 512 * (jj // 512) + 4 * (jj % 128) + ((jj // 128) % 4)
    gu_perm = np.concatenate([gate_perm, IDIM + gate_perm])

    f32 = lambda a: np.asarray(a, dtype=np.float32)
    x_dev = np.ascontiguousarray(x.transpose(0, 2, 1)[:, pi, :])

    shared = {
        "wqT": np.ascontiguousarray(f32(inputs["L0_attn_wq"]).T[pi, :]),
        "wkT": np.ascontiguousarray(f32(inputs["L0_attn_wk"]).T[pi, :]),
        "wvT": np.ascontiguousarray(f32(inputs["L0_attn_wv"]).T[pi, :]),
        "woT": np.ascontiguousarray(f32(inputs["L0_attn_wo"]).T[:, pi]),
        "fq1T": np.ascontiguousarray(f32(inputs["L0_attn_fq1_w"]).T),
        "fq2T": np.ascontiguousarray(f32(inputs["L0_attn_fq2_w"]).T),
        "fk1T": np.ascontiguousarray(f32(inputs["L0_attn_fk1_w"]).T),
        "fk2T": np.ascontiguousarray(f32(inputs["L0_attn_fk2_w"]).T),
        "bq1": f32(inputs["L0_attn_fq1_b"]).reshape(128, 1).copy(),
        "bq2": f32(inputs["L0_attn_fq2_b"]).reshape(128, 1).copy(),
        "bk1": f32(inputs["L0_attn_fk1_b"]).reshape(128, 1).copy(),
        "bk2": f32(inputs["L0_attn_fk2_b"]).reshape(128, 1).copy(),
        "normw": f32(inputs["L0_attn_norm_w"]).reshape(128, 1).copy(),
        "gup": np.ascontiguousarray(
            np.asarray(inputs["L0_ffn_gate_up_packed"]).view(np.uint16)[gu_perm, :].T),
        "gus": np.ascontiguousarray(
            np.repeat(np.asarray(inputs["L0_ffn_gate_up_scales"]), 32, axis=0)[:, gu_perm]),
        "dnp": np.ascontiguousarray(
            np.asarray(inputs["L0_ffn_down_packed"]).view(np.uint16)[pi, :].T),
        "dns": np.ascontiguousarray(
            np.repeat(np.asarray(inputs["L0_ffn_down_scales"]), 32, axis=0)[:, pi]),
        "ident": np.eye(128, dtype=np.float32),
    }
    m = np.zeros((SCH, SCH), np.float32)
    m[0:64, 64:128] = 1.0
    m[0:64, 0:64] = np.triu(np.ones((64, 64), np.float32))
    m[64:128, 64:128] = np.triu(np.ones((64, 64), np.float32))
    shared["mask4"] = np.ascontiguousarray(np.tile(m, (1, 4)))

    per_core = []
    for c in range(n_cores):
        b, seg = c // segs, c % segs
        m8 = np.zeros((128, n_cores), np.float32)
        for rr in range(n_cores):
            if rr // segs == b and rr % segs < seg:
                m8[:, rr] = 1.0
        per_core.append({
            "xs": np.ascontiguousarray(x_dev[b, :, seg * T:(seg + 1) * T]),
            "m8": m8,
        })
    return shared, per_core, pi


class Sem:
    __slots__ = ("h", "n")

    def __init__(self, mk, name):
        self.h = mk.ostack.enter_context(mk.nc.semaphore(name))
        self.n = 0

    def inc(self, instr, k=1):
        instr.then_inc(self.h, k)
        self.n += k
        return (self, self.n)

    def tag(self):
        return (self, self.n)


def wait(eng, tag):
    if tag is not None:
        eng.wait_ge(tag[0].h, tag[1])


class MK:

    def __init__(self, n_cores=8, T=2048, dbg=()):
        self.n_cores = n_cores
        self.dbg = set(dbg)
        self.T = T
        self.NG = T // 512
        nc = bass.Bass("TRN2", target_bir_lowering=False, detect_race_conditions=False)
        self.nc = nc
        self.ostack = ExitStack()

        i = lambda name, shape, dt: nc.dram_tensor(name, shape, dt, kind="ExternalInput").ap()
        self.xs = i("xs", [D, T], F32)
        self.m8 = i("m8", [128, n_cores], F32)
        self.wqT = i("wqT", [D, D], F32)
        self.wkT = i("wkT", [D, D], F32)
        self.wvT = i("wvT", [D, D], F32)
        self.woT = i("woT", [D, D], F32)
        self.fq1T = i("fq1T", [128, 128], F32)
        self.fq2T = i("fq2T", [128, 128], F32)
        self.fk1T = i("fk1T", [128, 128], F32)
        self.fk2T = i("fk2T", [128, 128], F32)
        self.bq1 = i("bq1", [128, 1], F32)
        self.bq2 = i("bq2", [128, 1], F32)
        self.bk1 = i("bk1", [128, 1], F32)
        self.bk2 = i("bk2", [128, 1], F32)
        self.normw = i("normw", [128, 1], F32)
        self.mask4 = i("mask4", [SCH, 4 * SCH], F32)
        self.ident_in = i("ident", [128, 128], F32)
        self.gup = i("gup", [GW, 2 * IDIM], U16)
        self.gus = i("gus", [GW, 2 * IDIM], F16)
        self.dnp = i("dnp", [DW, D], U16)
        self.dns = i("dns", [DW, D], F16)

        self.out = nc.dram_tensor("out", [T, D], F32, kind="ExternalOutput").ap()

        d = lambda name, shape, dt: nc.dram_tensor(
            name, shape, dt,
            kind=("ExternalOutput" if name in self.dbg else "Internal")).ap()
        self.d_wq16 = d("d_wq16", [D, D], F16)
        self.d_wk16 = d("d_wk16", [D, D], F16)
        self.d_wv16 = d("d_wv16", [D, D], F16)
        self.d_wo16 = d("d_wo16", [D, D], F16)
        self.d_kfm = d("d_kfm", [D, T], F16)
        self.d_ktm = d("d_ktm", [T, D], F16)
        self.d_vtm = d("d_vtm", [T, D], F16)
        self.d_xp = d("d_xp", [T, D], F16)
        self.d_xh2 = d("d_xh2", [D, T], F16)
        self.d_xh = d("d_xh", [D, T], F16)
        self.d_h = d("d_h", [IDIM, T], F16)
        self.d_wdn = d("d_wdn", [2 * IDIM, D // 2], F16)
        self.d_contrib = d("d_contrib", [128, D], F32)
        self.d_gath = d("d_gath", [128 * n_cores, D], F32)

        osb = lambda name, shape, dt: self.ostack.enter_context(nc.sbuf_tensor(name, shape, dt))
        self.ones = osb("ones", [128, 128], F16)
        self.ident32 = osb("ident32", [128, 128], F32)
        self.ident16 = osb("ident16", [128, 128], F16)
        self.mask4_sb = osb("mask4_sb", [SCH, 4 * SCH], F32)
        self.m8_sb = osb("m8_sb", [128, n_cores], F32)
        self.consts = osb("consts", [128, 8], F32)
        self.fqk = osb("fqk", [128, 512], F16)
        self.rows = osb("rows", [1, 1536], F32)

        s = lambda name: Sem(self, name)
        self.sem = s
        self.ld = s("ld")
        self.ldg = s("ldg")
        self.stq = s("stq")
        self.sto = s("sto")
        self.pe = s("pe")
        self.dv = s("dv")
        self.ac = s("ac")
        self.gp = s("gp")
        self.cc = s("cc")
        self.collective_done = None

    def build(self):
        nc = self.nc
        with ExitStack() as ph12:
            sb12 = lambda n_, s_, d_: ph12.enter_context(nc.sbuf_tensor(n_, s_, d_))
            self.x16 = [sb12("x16_0", [128, NJ * 512], F16), None]
            self.xh = sb12("xh", [128, NJ * 512], F16)
            self.mrs = sb12("mrs", [1, self.T], F16)
            self.rstd1 = sb12("rstd1", [1, self.T], F16)
            self.S_master = sb12("S_master", [128, D], F32)
            self.S0_16 = sb12("S0_16", [128, D], F16)
            self.gath_sb = sb12("gath_sb", [128, D], F32)
            self.gtmp = sb12("gtmp", [128, D], F32)
            self.S0_32 = sb12("S0_32", [128, D], F32)
            self.startup()
            with ExitStack() as ph1:
                self.phase1(ph1)
            self.barrier()
            self.exchange()
            with ExitStack() as ph2:
                self.phase2(ph2)
            self.barrier()
        with ExitStack() as ph3:
            self.phase3(ph3)
        self.barrier()
        with ExitStack() as ph4:
            self.phase4(ph4)
        self.finale()
        self.ostack.close()
        return nc

    def barrier(self):
        self.nc.all_engine_barrier()

    def psum(self, st, name, shape, dt=F32):
        return st.enter_context(self.nc.psum_tensor(name, shape, dt))

    def sbuf(self, st, name, shape, dt):
        return st.enter_context(self.nc.sbuf_tensor(name, shape, dt))

    def startup(self):
        nc = self.nc
        DV, SY, GP = nc.vector, nc.sync, nc.gpsimd
        DV.memset(self.ones[:, :], 1.0)
        DV.memset(self.S_master[:, :], 0.0)
        self.ld.inc(SY.dma_start(self.ident32[:, :], self.ident_in[:, :]), 16)
        self.ld.inc(SY.dma_start(self.mask4_sb[:, :], self.mask4[:, :]), 16)
        self.ld.inc(SY.dma_start(self.m8_sb[:, :], self.m8[:, :]), 16)
        self.ld.inc(SY.dma_start(self.consts[:, 0:1], self.bq1[:, :]), 16)
        self.ld.inc(SY.dma_start(self.consts[:, 1:2], self.bq2[:, :]), 16)
        self.ld.inc(SY.dma_start(self.consts[:, 2:3], self.bk1[:, :]), 16)
        self.ld.inc(SY.dma_start(self.consts[:, 3:4], self.bk2[:, :]), 16)
        t_c = self.ld.inc(SY.dma_start(self.consts[:, 4:5], self.normw[:, :]), 16)
        self.ldg.inc(GP.dma_start(self.ident16[:, :], self.ident_in[:, :]), 16)
        self.ldg.inc(GP.dma_start(self.fqk[:, 0:128], self.fq1T[:, :]), 16)
        self.ldg.inc(GP.dma_start(self.fqk[:, 128:256], self.fq2T[:, :]), 16)
        self.ldg.inc(GP.dma_start(self.fqk[:, 256:384], self.fk1T[:, :]), 16)
        t_g = self.ldg.inc(GP.dma_start(self.fqk[:, 384:512], self.fk2T[:, :]), 16)
        self.ldg.inc(GP.dma_start(self.d_wk16[:, :], self.wkT[:, :]), 16)
        self.t_wkv = self.ldg.inc(GP.dma_start(self.d_wv16[:, :], self.wvT[:, :]), 16)
        self.ldg.inc(GP.dma_start(self.d_wq16[:, :], self.wqT[:, :]), 16)
        self.t_wcast = self.ldg.inc(GP.dma_start(self.d_wo16[:, :], self.woT[:, :]), 16)
        wait(DV, t_g)
        wait(DV, t_c)
        sc = float(DK ** -0.5)
        DV.tensor_scalar(self.fqk[:, 0:128], self.fqk[:, 0:128], sc, None, ALU.mult)
        self.t_const = self.dv.inc(
            DV.tensor_scalar(self.consts[:, 0:1], self.consts[:, 0:1], sc, None, ALU.mult))

    def ln_rows(self, sums_ap, sqs_ap, eps, rstd_dst16, mrs_dst16, wtags=(), sq_scale=1.0):
        nc = self.nc
        DV, AC = nc.vector, nc.scalar
        rows = self.rows
        for t in wtags:
            wait(DV, t)
        DV.tensor_scalar(rows[0:1, 0:512], sums_ap, 1.0 / D, None, ALU.mult)
        DV.tensor_tensor(rows[0:1, 512:1024], rows[0:1, 0:512], rows[0:1, 0:512], ALU.mult)
        DV.tensor_scalar(rows[0:1, 512:1024], rows[0:1, 512:1024], eps, None, ALU.subtract)
        t_var = self.dv.inc(DV.scalar_tensor_tensor(
            rows[0:1, 1024:1536], sqs_ap, sq_scale / D, rows[0:1, 512:1024],
            ALU.mult, ALU.subtract))
        wait(AC, t_var)
        AC.activation(rows[0:1, 512:1024], rows[0:1, 1024:1536], AF.Ln)
        t_rstd = self.ac.inc(AC.activation(rstd_dst16, rows[0:1, 512:1024], AF.Exp, scale=-0.5))
        wait(DV, t_rstd)
        t_mrs = self.dv.inc(DV.tensor_tensor(mrs_dst16, rows[0:1, 0:512], rstd_dst16, ALU.mult))
        return t_mrs

    def bcast2(self, src_a16, src_b16, ps_a, ps_b, wtag):
        nc = self.nc
        PE = nc.tensor
        wait(PE, wtag)
        PE.matmul(ps_a, self.ones[0:1, :], src_a16, start=True, stop=True)
        return self.pe.inc(PE.matmul(ps_b, self.ones[0:1, :], src_b16, start=True, stop=True))

    def phase1(self, st):
        nc = self.nc
        PE, DV, AC, SY, GP = nc.tensor, nc.vector, nc.scalar, nc.sync, nc.gpsimd
        T = self.T

        self.x16[1] = self.sbuf(st, "x16_1", [128, NJ * 512], F16)
        xsq = self.sbuf(st, "xsq", [128, 512], F16)
        wvb = [self.sbuf(st, f"wvb{k}", [128, NJ * 512], F16) for k in range(2)]
        wkb = [self.sbuf(st, f"wkb{k}", [128, NJ * 128], F16) for k in range(2)]
        kr = self.sbuf(st, "kr", [128, 512], F16)
        k1s = self.sbuf(st, "k1s", [128, 512], F16)
        k2s = self.sbuf(st, "k2s", [128, 512], F16)
        kfmb = self.sbuf(st, "kfmb", [128, 512], F16)
        ktm_asm = self.sbuf(st, "ktm_asm", [128, 4 * 2048], F16)
        vtm_asm = self.sbuf(st, "vtm_asm", [128, 4 * 2048], F16)
        tot_ev = self.sbuf(st, "tot_ev", [128, 2048], F32)
        ps_tot = [self.psum(st, f"ps_tot{i}", [128, 512]) for i in range(4)]
        ps_a = self.psum(st, "ps_a", [128, 512])
        ps_b = self.psum(st, "ps_b", [128, 512])
        ps_c = self.psum(st, "ps_c", [128, 512])
        ps_t16 = self.psum(st, "ps_t16", [128, 512], F16)

        ld_wv = self.sem("ld_wv")
        ld_wk = self.sem("ld_wk")

        def load_wv(ec, buf):
            wait(SY, self.t_wkv)
            return ld_wv.inc(SY.dma_start(
                wvb[buf][:, :].rearrange("p (j o) -> p j o", j=NJ),
                self.d_wv16.rearrange("(j p) o -> p j o", p=128)[:, :, ec * 512:(ec + 1) * 512]), 16)

        def load_wk(h, buf):
            wait(SY, self.t_wkv)
            return ld_wk.inc(SY.dma_start(
                wkb[buf][:, :].rearrange("p (j e) -> p j e", j=NJ),
                self.d_wk16.rearrange("(j p) o -> p j o", p=128)[:, :, h * 128:(h + 1) * 128]), 16)

        t_x = [None, None]
        t_x[0] = self.ldg.inc(GP.dma_start(
            self.x16[0][:, :].rearrange("p (j t) -> p j t", j=NJ),
            self.xs.rearrange("(j p) t -> p j t", p=128)[:, :, 0:512]), 16)
        t_wv = [load_wv(0, 0), None]
        t_wk = [load_wk(0, 0), None]

        t_xh_used = None
        t_xh_spill = None
        t_kfm_spill = None
        t_xb_free = [None, None]
        t_spill_v = None
        t_spill_kt = None
        t_wv_used = [None, None]
        t_wk_used = [None, None]

        for g in range(self.NG):
            t0 = g * 512
            if g + 1 < self.NG:
                wait(GP, t_xb_free[(g + 1) % 2])
                t_x[(g + 1) % 2] = self.ldg.inc(GP.dma_start(
                    self.x16[(g + 1) % 2][:, :].rearrange("p (j t) -> p j t", j=NJ),
                    self.xs.rearrange("(j p) t -> p j t", p=128)[:, :, t0 + 512:t0 + 1024]), 16)
            xb = self.x16[g % 2]

            wait(PE, t_x[g % 2])
            wait(PE, self.t_const)
            if g > 0:
                wait(DV, t_xh_spill)
            for j in range(NJ):
                ip = PE.matmul(ps_a[0:1, :], self.ones[:, 0:1], xb[:, j * 512:(j + 1) * 512],
                               start=(j == 0), stop=(j == NJ - 1))
            t_sx = self.pe.inc(ip)
            wait(AC, t_x[g % 2])
            t_sq_mm = None
            for j in range(NJ):
                wait(AC, t_sq_mm)
                t_sq = self.ac.inc(AC.activation(xsq[:, :], xb[:, j * 512:(j + 1) * 512], AF.Square))
                wait(PE, t_sq)
                t_sq_mm = self.pe.inc(PE.matmul(ps_b[0:1, :], self.ones[:, 0:1], xsq[:, :],
                                                start=(j == 0), stop=(j == NJ - 1)))
            t_mrs = self.ln_rows(ps_a[0:1, :], ps_b[0:1, :], LN_EPS,
                                 self.rstd1[0:1, t0:t0 + 512], self.mrs[0:1, t0:t0 + 512],
                                 wtags=(t_sx, t_sq_mm))
            t_bc = self.bcast2(self.rstd1[0:1, t0:t0 + 512], self.mrs[0:1, t0:t0 + 512],
                               ps_a[:, :], ps_b[:, :], t_mrs)
            wait(DV, t_bc)
            wait(DV, t_xh_used)
            for j in range(NJ):
                DV.tensor_tensor(self.xh[:, j * 512:(j + 1) * 512], xb[:, j * 512:(j + 1) * 512],
                                 ps_a[:, :], ALU.mult)
            for j in range(NJ):
                iv = DV.tensor_tensor(self.xh[:, j * 512:(j + 1) * 512],
                                      self.xh[:, j * 512:(j + 1) * 512], ps_b[:, :], ALU.subtract)
            t_xh = self.dv.inc(iv)
            t_xb_free[g % 2] = t_xh
            wait(SY, t_xh)
            t_xh_spill = self.stq.inc(SY.dma_start(
                self.d_xh.rearrange("(j p) t -> p j t", p=128)[:, :, t0:t0 + 512],
                self.xh[:, :].rearrange("p (j t) -> p j t", j=NJ)), 16)

            wait(PE, t_xh)
            t_v_evac = [None, None]
            for ec in range(4):
                buf = ec % 2
                nec, nbuf = ec + 1, (ec + 1) % 2
                if nec < 4:
                    wait(SY, t_wv_used[nbuf])
                    t_wv[nbuf] = load_wv(nec, nbuf)
                elif g + 1 < self.NG:
                    wait(SY, t_wv_used[nbuf])
                    t_wv[nbuf] = load_wv(0, nbuf)
                wait(PE, t_wv[buf])
                for sub in range(4):
                    pq = ps_a if sub % 2 == 0 else ps_b
                    wait(PE, t_v_evac[sub % 2])
                    for j in range(NJ):
                        ipv = PE.matmul(
                            pq[:, :],
                            self.xh[:, j * 512 + sub * 128:j * 512 + (sub + 1) * 128],
                            wvb[buf][:, j * 512:(j + 1) * 512],
                            start=(j == 0), stop=(j == NJ - 1))
                    t_vmm = self.pe.inc(ipv)
                    wait(DV, t_vmm)
                    t_v_evac[sub % 2] = self.dv.inc(DV.tensor_copy(
                        vtm_asm[:, sub * 2048 + ec * 512:sub * 2048 + (ec + 1) * 512], pq[:, :]))
                t_wv_used[buf] = self.pe.tag()
            wait(SY, t_v_evac[0])
            wait(SY, t_v_evac[1])
            wait(SY, t_spill_v)
            t_spill_v = self.stq.inc(SY.dma_start(
                self.d_vtm.rearrange("(s p) o -> p s o", p=128)[:, 4 * g:4 * (g + 1), :],
                vtm_asm[:, :].rearrange("p (s o) -> p s o", s=4)), 16)

            t_kr_used = None
            t_kmap_used = None
            t_kfm_used = None
            t_ktr_evac = None
            for h in range(H):
                buf = h % 2
                if h + 1 < H:
                    wait(SY, t_wk_used[(h + 1) % 2])
                    t_wk[(h + 1) % 2] = load_wk(h + 1, (h + 1) % 2)
                elif g + 1 < self.NG:
                    wait(SY, t_wk_used[(h + 1) % 2])
                    t_wk[(h + 1) % 2] = load_wk(0, (h + 1) % 2)
                wait(PE, t_kr_used)
                wait(PE, t_wk[buf])
                for j in range(NJ):
                    ik = PE.matmul(ps_c[:, :],
                                   wkb[buf][:, j * 128:(j + 1) * 128],
                                   self.xh[:, j * 512:(j + 1) * 512],
                                   start=(j == 0), stop=(j == NJ - 1))
                t_kraw = self.pe.inc(ik)
                t_wk_used[buf] = t_kraw
                wait(AC, t_kraw)
                t_kr = self.ac.inc(AC.activation(kr[:, :], ps_c[:, :], AF.Copy))
                t_kr_used = t_kr
                wait(PE, t_kr)
                wait(PE, t_kmap_used)
                PE.matmul(ps_a[:, :], self.fqk[:, 256:384], kr[:, :], start=True, stop=True)
                t_kmaps = self.pe.inc(PE.matmul(ps_b[:, :], self.fqk[:, 384:512], kr[:, :],
                                                start=True, stop=True))
                wait(AC, t_kmaps)
                AC.activation(k1s[:, :], ps_a[:, :], AF.Identity, bias=self.consts[:, 2:3])
                t_kbias = self.ac.inc(AC.activation(k2s[:, :], ps_b[:, :], AF.Identity,
                                                    bias=self.consts[:, 3:4]))
                t_kmap_used = t_kbias
                wait(DV, t_kbias)
                wait(DV, t_kfm_used)
                wait(DV, t_kfm_spill)
                t_kfm = self.dv.inc(DV.tensor_tensor(kfmb[:, :], k1s[:, :], k2s[:, :], ALU.mult))
                wait(SY, t_kfm)
                t_kfm_spill = self.stq.inc(SY.dma_start(
                    self.d_kfm[h * 128:(h + 1) * 128, t0:t0 + 512], kfmb[:, :]), 16)
                wait(PE, t_kfm)
                wait(PE, t_ktr_evac)
                for sub in range(4):
                    it = PE.transpose(ps_t16[:, sub * 128:(sub + 1) * 128],
                                      kfmb[:, sub * 128:(sub + 1) * 128], self.ident16[:, :])
                t_ktr = self.pe.inc(it)
                t_kfm_used = t_ktr
                wait(DV, t_ktr)
                wait(DV, t_spill_kt)
                t_ktr_evac = self.dv.inc(DV.tensor_copy(
                    ktm_asm[:, :].rearrange("p (s o) -> p s o", s=4)[:, :, h * 128:(h + 1) * 128],
                    ps_t16[:, :].rearrange("p (s e) -> p s e", s=4)))
            wait(SY, t_ktr_evac)
            t_spill_kt = self.stq.inc(SY.dma_start(
                self.d_ktm.rearrange("(s p) o -> p s o", p=128)[:, 4 * g:4 * (g + 1), :],
                ktm_asm[:, :].rearrange("p (s o) -> p s o", s=4)), 16)

            wait(PE, t_ktr_evac)
            for sub in range(4):
                for h in range(H):
                    itt = PE.matmul(
                        ps_tot[h // 4][:, (h % 4) * 128:(h % 4 + 1) * 128],
                        ktm_asm[:, sub * 2048 + h * 128:sub * 2048 + (h + 1) * 128],
                        vtm_asm[:, sub * 2048 + h * 128:sub * 2048 + (h + 1) * 128],
                        start=(g == 0 and sub == 0 and h % 4 == 0),
                        stop=(g == self.NG - 1 and sub == 3 and h % 4 == 3))
            t_tot = self.pe.inc(itt)
            t_xh_used = t_tot

        wait(DV, t_tot)
        for q in range(4):
            ie = DV.tensor_copy(tot_ev[:, q * 512:(q + 1) * 512], ps_tot[q][:, :])
        t_ev = self.dv.inc(ie)
        wait(SY, t_ev)
        self.stq.inc(SY.dma_start(self.d_contrib[:, :], tot_ev[:, :]), 16)
        self.x16[1] = None

    def exchange(self):
        nc = self.nc
        GP = nc.gpsimd
        gp_sem = self.sem("gp_ex")
        gld = self.sem("gp_gld")
        wait(GP, self.stq.tag())
        t_cc = self.cc.inc(GP.collective_compute(
            "AllGather", ALU.bypass, replica_groups=[list(range(self.n_cores))],
            ins=[self.d_contrib[:, :]], outs=[self.d_gath[:, :]]), 1)
        gsb, acc = self.gath_sb, self.S0_32
        GP.memset(acc[:, :], 0.0)
        wait(GP, t_cc)
        for r in range(self.n_cores):
            t_g = gld.inc(GP.dma_start(gsb[:, :], self.d_gath[128 * r:128 * (r + 1), :]), 16)
            wait(GP, t_g)
            GP.tensor_scalar(self.gtmp[:, :], gsb[:, :], self.m8_sb[:, r:r + 1], None, ALU.mult)
            GP.tensor_tensor(acc[:, :], acc[:, :], self.gtmp[:, :], ALU.add)
        self.collective_done = gp_sem.inc(GP.tensor_copy(self.S0_16[:, :], acc[:, :]))

    def phase2(self, st):
        nc = self.nc
        PE, DV, AC, SY, GP = nc.tensor, nc.vector, nc.scalar, nc.sync, nc.gpsimd
        T = self.T
        sb = lambda n_, s_, d_: self.sbuf(st, n_, s_, d_)

        kfm_r = [sb(f"kfm_r{k}", [128, 4 * 512], F16) for k in range(2)]
        ktm_r = [sb(f"ktm_r{k}", [128, 4 * 512], F16) for k in range(2)]
        vtm_r = [sb(f"vtm_r{k}", [128, 4 * 512], F16) for k in range(2)]
        qraw = sb("qraw", [128, 4 * 512], F16)
        q1s = sb("q1s", [128, 4 * 512], F16)
        q2s = sb("q2s", [128, 4 * 512], F16)
        qfin = sb("qfin", [128, 4 * 512], F16)
        onorm = sb("onorm", [128, H * 512], F16)
        wq_rb = [sb("wq_rb0", [128, NJ * 512], F16)]
        wo_ob = [sb(f"wo_ob{k}", [128, NJ * 128], F16) for k in range(2)]
        S16 = sb("S16", [128, 512], F16)
        sTs = [sb(f"sTs{k}", [SCH, 4 * SCH], F16) for k in range(2)]
        rbc_sb = sb("rbc_sb", [128, 512], F32)
        o32 = self.gtmp
        osq = sb("osq", [128, 512], F16)
        xp = sb("xp", [128, NJ * 512], F16)
        xpt = sb("xpt", [128, 2048], F16)
        xh2b = sb("xh2b", [128, 512], F16)

        ps_o = [self.psum(st, f"ps_o{i}", [128, 512]) for i in range(4)]
        ps_st = self.psum(st, "ps_st", [128, 512])
        ps_x = self.psum(st, "ps_x", [128, 512])
        ps_w = self.psum(st, "ps_w", [128, 512])
        ps_w2 = self.psum(st, "ps_w2", [128, 512])

        ld_kv = self.sem("ld_kv")
        ld_wq = self.sem("ld_wq")
        ld_wo = self.sem("ld_wo")
        ld_x2 = self.sem("ld_x2")

        def load_round(g, r, buf):
            t0 = g * 512
            ld_kv.inc(SY.dma_start(
                kfm_r[buf][:, :].rearrange("p (h t) -> p h t", h=4),
                self.d_kfm.rearrange("(h p) t -> p h t", p=128)[:, 4 * r:4 * (r + 1), t0:t0 + 512]), 16)
            ld_kv.inc(SY.dma_start(
                ktm_r[buf][:, :].rearrange("p (s o) -> p s o", s=4),
                self.d_ktm.rearrange("(s p) o -> p s o", p=128)
                [:, 4 * g:4 * (g + 1), r * 512:(r + 1) * 512]), 16)
            return ld_kv.inc(SY.dma_start(
                vtm_r[buf][:, :].rearrange("p (s o) -> p s o", s=4),
                self.d_vtm.rearrange("(s p) o -> p s o", p=128)
                [:, 4 * g:4 * (g + 1), r * 512:(r + 1) * 512]), 16)

        t_wq_used = [None]

        def load_wq(r, buf):
            wait(SY, self.t_wcast)
            wait(SY, t_wq_used[0])
            return ld_wq.inc(SY.dma_start(
                wq_rb[0][:, :].rearrange("p (j t) -> p j t", j=NJ),
                self.d_wq16.rearrange("(j p) o -> p j o", p=128)[:, :, r * 512:(r + 1) * 512]), 16)

        wait(SY, self.stq.tag())
        xb = self.x16[0]
        ld_xh = self.sem("ld_xh")
        t_x = self.ldg.inc(GP.dma_start(
            xb[:, :].rearrange("p (j t) -> p j t", j=NJ),
            self.xs.rearrange("(j p) t -> p j t", p=128)[:, :, 0:512]), 16)
        t_xh = ld_xh.inc(SY.dma_start(
            self.xh[:, :].rearrange("p (j t) -> p j t", j=NJ),
            self.d_xh.rearrange("(j p) t -> p j t", p=128)[:, :, 0:512]), 16)
        t_kv = [load_round(0, 0, 0), None]
        t_wq = [load_wq(0, 0)]
        t_kv_used = [None, None]

        t_ps_w_free = None
        t_ps_w2_free = None
        t_ps_o_free = [None] * 4
        t_ps_x_free = None
        t_ps_st_free = None
        t_qfin_used = None
        t_onorm_used = None
        t_xh_used = None
        t_sT_used = [None, None]
        t_xb_free = None

        for g in range(self.NG):
            t0 = g * 512
            for r in range(4):
                buf = r % 2
                nbuf = (r + 1) % 2
                if r < 3:
                    wait(SY, t_kv_used[nbuf])
                    t_kv[nbuf] = load_round(g, r + 1, nbuf)
                elif g + 1 < self.NG:
                    wait(SY, t_kv_used[nbuf])
                    t_kv[nbuf] = load_round(g + 1, 0, nbuf)
                wqb = wq_rb[0]
                wait(PE, t_xh)
                wait(DV, t_x)
                wait(PE, t_wq[0])
                for hh in range(4):
                    wait(PE, t_ps_o_free[hh])
                    for j in range(NJ):
                        iq = PE.matmul(ps_o[hh][:, :],
                                       wqb[:, j * 512 + hh * 128:j * 512 + (hh + 1) * 128],
                                       self.xh[:, j * 512:(j + 1) * 512],
                                       start=(j == 0), stop=(j == NJ - 1))
                    tq = self.pe.inc(iq)
                    wait(AC, tq)
                    ta = self.ac.inc(AC.activation(qraw[:, hh * 512:(hh + 1) * 512],
                                                   ps_o[hh][:, :], AF.Copy))
                    t_ps_o_free[hh] = ta
                t_wq_used[0] = self.pe.tag()
                if r < 3:
                    t_wq[0] = load_wq(r + 1, 0)
                elif g + 1 < self.NG:
                    t_wq[0] = load_wq(0, 0)
                for hh in range(4):
                    wait(PE, t_ps_o_free[hh])
                    im = PE.matmul(ps_o[hh][:, :], self.fqk[:, 0:128],
                                   qraw[:, hh * 512:(hh + 1) * 512], start=True, stop=True)
                    tm = self.pe.inc(im)
                    wait(AC, tm)
                    ta = self.ac.inc(AC.activation(q1s[:, hh * 512:(hh + 1) * 512],
                                                   ps_o[hh][:, :], AF.Identity,
                                                   bias=self.consts[:, 0:1]))
                    t_ps_o_free[hh] = ta
                for hh in range(4):
                    wait(PE, t_ps_o_free[hh])
                    im = PE.matmul(ps_o[hh][:, :], self.fqk[:, 128:256],
                                   qraw[:, hh * 512:(hh + 1) * 512], start=True, stop=True)
                    tm = self.pe.inc(im)
                    wait(AC, tm)
                    ta = self.ac.inc(AC.activation(q2s[:, hh * 512:(hh + 1) * 512],
                                                   ps_o[hh][:, :], AF.Identity,
                                                   bias=self.consts[:, 1:2]))
                    t_ps_o_free[hh] = ta
                wait(DV, self.ac.tag())
                wait(DV, t_qfin_used)
                for hh in range(4):
                    iv = DV.tensor_tensor(qfin[:, hh * 512:(hh + 1) * 512],
                                          q1s[:, hh * 512:(hh + 1) * 512],
                                          q2s[:, hh * 512:(hh + 1) * 512], ALU.mult)
                t_qfin = self.dv.inc(iv)

                t_update = None
                t_madd = None
                wait(PE, t_kv[buf])
                wait(PE, t_qfin)
                for scn in range(4):
                    wait(DV, t_madd)
                    t_scopy = self.dv.inc(DV.tensor_copy(
                        S16[:, :], self.S_master[:, r * 512:(r + 1) * 512]))
                    wait(PE, t_sT_used[scn % 2])
                    wait(PE, t_ps_x_free)
                    for hh in range(4):
                        isT = PE.matmul(
                            ps_x[0:SCH, hh * SCH:(hh + 1) * SCH],
                            kfm_r[buf][:, hh * 512 + scn * 128:hh * 512 + (scn + 1) * 128],
                            qfin[:, hh * 512 + scn * 128:hh * 512 + (scn + 1) * 128],
                            start=True, stop=True)
                    t_sT = self.pe.inc(isT)
                    wait(DV, t_sT)
                    t_sTe = self.dv.inc(DV.tensor_tensor(sTs[scn % 2][:, :], ps_x[0:SCH, :],
                                                         self.mask4_sb[:, :], ALU.mult))
                    t_ps_x_free = t_sTe
                    wait(PE, t_scopy)
                    wait(PE, t_sTe)
                    for hh in range(4):
                        PE.matmul(ps_o[hh][:, scn * 128:(scn + 1) * 128],
                                  S16[:, hh * 128:(hh + 1) * 128],
                                  qfin[:, hh * 512 + scn * 128:hh * 512 + (scn + 1) * 128],
                                  start=(scn == 0), stop=False)
                        PE.matmul(ps_o[hh][:, scn * 128:(scn + 1) * 128],
                                  vtm_r[buf][:, scn * 512 + hh * 128:scn * 512 + (hh + 1) * 128],
                                  sTs[scn % 2][:, hh * SCH:(hh + 1) * SCH],
                                  start=False, stop=False)
                    wait(PE, t_ps_st_free)
                    for hh in range(4):
                        iu = PE.matmul(ps_st[:, hh * 128:(hh + 1) * 128],
                                       ktm_r[buf][:, scn * 512 + hh * 128:scn * 512 + (hh + 1) * 128],
                                       vtm_r[buf][:, scn * 512 + hh * 128:scn * 512 + (hh + 1) * 128],
                                       start=(hh == 0), stop=(hh == 3))
                    t_update = self.pe.inc(iu)
                    t_sT_used[scn % 2] = t_update
                    wait(DV, t_update)
                    t_madd = self.dv.inc(DV.tensor_tensor(
                        self.S_master[:, r * 512:(r + 1) * 512], ps_st[:, :],
                        self.S_master[:, r * 512:(r + 1) * 512], ALU.add))
                    t_ps_st_free = t_madd
                t_kv_used[buf] = t_update

                wait(PE, self.collective_done)
                wait(PE, t_ps_st_free)
                wait(PE, t_ps_w_free)
                stat_slots = [(ps_st, 0), (ps_st, 64), (ps_w, 0), (ps_w, 64)]
                t_corrs = []
                for hh in range(4):
                    h = 4 * r + hh
                    ic = PE.matmul(ps_o[hh][:, :], self.S0_16[:, h * 128:(h + 1) * 128],
                                   qfin[:, hh * 512:(hh + 1) * 512], start=False, stop=True)
                    t_corrs.append(self.pe.inc(ic))
                t_qfin_used = self.pe.tag()
                t_o32 = []
                for hh in range(4):
                    wait(DV, t_corrs[hh])
                    tcp = self.dv.inc(DV.tensor_copy(o32[:, hh * 512:(hh + 1) * 512],
                                                     ps_o[hh][:, :]))
                    t_o32.append(tcp)
                    t_ps_o_free[hh] = tcp
                t_stats = []
                t_prev_stat = None
                for hh in range(4):
                    wait(AC, t_o32[hh])
                    wait(AC, t_prev_stat)
                    t_osq = self.ac.inc(AC.activation(osq[:, :], o32[:, hh * 512:(hh + 1) * 512],
                                                      AF.Square, scale=0.25))
                    wait(PE, t_osq)
                    bank, row = stat_slots[hh]
                    t_stat = self.pe.inc(PE.matmul(bank[row:row + 1, :], self.ones[:, 0:1],
                                                   osq[:, :], start=True, stop=True))
                    t_prev_stat = t_stat
                    t_stats.append((bank, row, t_stat))
                t_rbc_used = None
                t_last_row_read = None
                for hh in range(4):
                    h = 4 * r + hh
                    bank, row, t_stat = t_stats[hh]
                    wait(DV, t_stat)
                    t_ms = self.dv.inc(DV.tensor_scalar(
                        self.rows[0:1, 0:512], bank[row:row + 1, :],
                        16.0 / DK, ATT_EPS, ALU.mult, ALU.add))
                    t_last_row_read = t_ms
                    wait(AC, t_ms)
                    AC.activation(self.rows[0:1, 512:1024], self.rows[0:1, 0:512], AF.Ln)
                    t_rr = self.ac.inc(AC.activation(
                        self.rows[0:1, 1024:1280].bitcast(F16)[0:1, 0:512],
                        self.rows[0:1, 512:1024], AF.Exp, scale=-0.5))
                    wait(PE, t_rr)
                    wait(PE, t_ps_x_free)
                    t_rbc = self.pe.inc(PE.matmul(
                        ps_x[:, :], self.ones[0:1, :],
                        self.rows[0:1, 1024:1280].bitcast(F16)[0:1, 0:512],
                        start=True, stop=True))
                    wait(DV, t_rbc)
                    wait(DV, t_rbc_used)
                    t_rc = self.dv.inc(DV.tensor_copy(rbc_sb[:, :], ps_x[:, :]))
                    t_ps_x_free = t_rc
                    wait(DV, t_onorm_used)
                    t_on = self.dv.inc(DV.scalar_tensor_tensor(
                        onorm[:, h * 512:(h + 1) * 512], o32[:, hh * 512:(hh + 1) * 512],
                        self.consts[:, 4:5], rbc_sb[:, :], ALU.mult, ALU.mult))
                    t_rbc_used = t_on
                t_ps_st_free = t_last_row_read
                t_ps_w_free = t_last_row_read
            t_onorm = self.dv.tag()

            if g + 1 < self.NG:
                wait(SY, t_qfin_used)
                t_xh = ld_xh.inc(SY.dma_start(
                    self.xh[:, :].rearrange("p (j t) -> p j t", j=NJ),
                    self.d_xh.rearrange("(j p) t -> p j t", p=128)
                    [:, :, t0 + 512:t0 + 1024]), 16)

            t_wo = [None, None]
            wait(SY, self.t_wcast)
            t_wo[0] = ld_wo.inc(SY.dma_start(
                wo_ob[0][:, :].rearrange("p (h o) -> p h o", h=NJ),
                self.d_wo16.rearrange("(h p) o -> p h o", p=128)[:, :, 0:128]), 16)
            t_xp_evac = None
            t_c16mm = None
            t_sq2 = None
            t_wo_used = [None, None]
            ps_wo = [ps_w2, ps_w]
            t_ps_wo_free = [t_ps_w2_free, t_ps_w_free]
            for ot in range(NJ):
                if ot + 1 < NJ:
                    wait(SY, t_wo_used[(ot + 1) % 2])
                    t_wo[(ot + 1) % 2] = ld_wo.inc(SY.dma_start(
                        wo_ob[(ot + 1) % 2][:, :].rearrange("p (h o) -> p h o", h=NJ),
                        self.d_wo16.rearrange("(h p) o -> p h o", p=128)
                        [:, :, (ot + 1) * 128:(ot + 2) * 128]), 16)
                wait(PE, t_onorm)
                wait(PE, t_wo[ot % 2])
                wait(PE, t_ps_wo_free[ot % 2])
                pw = ps_wo[ot % 2]
                for h in range(H):
                    iw = PE.matmul(pw[:, :], wo_ob[ot % 2][:, h * 128:(h + 1) * 128],
                                   onorm[:, h * 512:(h + 1) * 512],
                                   start=(h == 0), stop=(h == H - 1))
                t_womm = self.pe.inc(iw)
                t_wo_used[ot % 2] = t_womm
                wait(DV, t_womm)
                t_xp_evac = self.dv.inc(DV.tensor_tensor(
                    xp[:, ot * 512:(ot + 1) * 512], pw[:, :],
                    xb[:, ot * 512:(ot + 1) * 512], ALU.add))
                t_ps_wo_free[ot % 2] = t_xp_evac
                wait(PE, t_xp_evac)
                t_c16mm = self.pe.inc(PE.matmul(ps_o[0][0:1, 0:512], self.ones[:, 0:1],
                                                xp[:, ot * 512:(ot + 1) * 512],
                                                start=(ot == 0), stop=(ot == NJ - 1)))
                wait(AC, t_xp_evac)
                wait(AC, t_sq2)
                t_sqa = self.ac.inc(AC.activation(osq[:, :], xp[:, ot * 512:(ot + 1) * 512],
                                                  AF.Square, scale=0.5))
                wait(PE, t_sqa)
                t_sq2 = self.pe.inc(PE.matmul(ps_o[1][0:1, 0:512], self.ones[:, 0:1], osq[:, :],
                                              start=(ot == 0), stop=(ot == NJ - 1)))
            t_ps_w2_free = t_ps_wo_free[0]
            t_ps_w_free = t_ps_wo_free[1]
            t_xb_free = t_xp_evac
            t_mrs2 = self.ln_rows(ps_o[0][0:1, 0:512], ps_o[1][0:1, 0:512], LN_EPS,
                                  self.rows[0:1, 1024:1280].bitcast(F16)[0:1, 0:512],
                                  self.rows[0:1, 1280:1536].bitcast(F16)[0:1, 0:512],
                                  wtags=(t_c16mm, t_sq2), sq_scale=4.0)
            t_bc2 = self.bcast2(self.rows[0:1, 1024:1280].bitcast(F16)[0:1, 0:512],
                                self.rows[0:1, 1280:1536].bitcast(F16)[0:1, 0:512],
                                ps_o[0][:, :], ps_o[1][:, :], t_mrs2)
            t_xh2_st = None
            t_xh2_dv = None
            for ot in range(NJ):
                wait(DV, t_bc2)
                wait(DV, t_xh2_st)
                DV.tensor_tensor(xh2b[:, :], xp[:, ot * 512:(ot + 1) * 512], ps_o[0][:, :],
                                 ALU.mult)
                t_xh2_dv = self.dv.inc(DV.tensor_tensor(xh2b[:, :], xh2b[:, :], ps_o[1][:, :],
                                                        ALU.subtract))
                wait(SY, t_xh2_dv)
                t_xh2_st = self.stq.inc(SY.dma_start(
                    self.d_xh2[ot * 128:(ot + 1) * 128, t0:t0 + 512], xh2b[:, :]), 16)
            t_ps_o_free[0] = t_xh2_dv
            t_ps_o_free[1] = t_xh2_dv
            if g + 1 < self.NG:
                wait(GP, t_xb_free)
                t_x = self.ldg.inc(GP.dma_start(
                    xb[:, :].rearrange("p (j t) -> p j t", j=NJ),
                    self.xs.rearrange("(j p) t -> p j t", p=128)
                    [:, :, t0 + 512:t0 + 1024]), 16)
            t_tr_evac = [None, None]
            t_xpt_st = None
            ps_x16r = [ps_x[:, 0:256].bitcast(F16), ps_o[3][:, 0:256].bitcast(F16)]
            first_tr = True
            for tc in range(4):
                for j4 in range(4):
                    par = j4 % 2
                    wait(PE, t_tr_evac[par])
                    if first_tr:
                        wait(PE, t_ps_x_free)
                        wait(PE, t_ps_o_free[3])
                        first_tr = False
                    for jj in range(4):
                        j = j4 * 4 + jj
                        itr = PE.transpose(ps_x16r[par][:, jj * 128:(jj + 1) * 128],
                                           xp[:, j * 512 + tc * 128:j * 512 + (tc + 1) * 128],
                                           self.ident16[:, :])
                    t_tr = self.pe.inc(itr)
                    wait(DV, t_tr)
                    wait(DV, t_xpt_st)
                    t_tr_evac[par] = self.dv.inc(DV.tensor_copy(
                        xpt[:, j4 * 512:(j4 + 1) * 512], ps_x16r[par][:, 0:512]))
                wait(SY, t_tr_evac[0])
                wait(SY, t_tr_evac[1])
                t_xpt_st = self.stq.inc(SY.dma_start(
                    self.d_xp[t0 + tc * 128:t0 + (tc + 1) * 128, :], xpt[:, :]), 16)
            t_ps_x_free = t_tr_evac[0]
            t_ps_o_free[3] = t_tr_evac[1]
            t_xh_used = t_qfin_used
            t_onorm_used = self.pe.tag()

    def phase3(self, st):
        nc = self.nc
        PE, DV, AC, SY, GP = nc.tensor, nc.vector, nc.scalar, nc.sync, nc.gpsimd
        T = self.T
        sb = lambda n_, s_, d_: self.sbuf(st, n_, s_, d_)

        xh2 = sb("xh2_full", [128, NJ * T], F16)
        wait(SY, self.stq.tag())
        t_xh2 = self.ld.inc(SY.dma_start(
            xh2[:, :].rearrange("p (j t) -> p j t", j=NJ),
            self.d_xh2.rearrange("(j p) t -> p j t", p=128)), 16)
        gpk = [sb(f"gpk{k}", [128, 4 * 256], U16) for k in range(2)]
        gsc = [sb(f"gsc{k}", [128, 4 * 256], F16) for k in range(2)]
        wtil = [sb(f"wtil{k}", [128, 16 * 256], F16) for k in range(2)]
        hsil = sb("hsil", [128, T], F16)
        up16 = sb("up16", [128, T], F16)
        hbuf = sb("hbuf", [128, T], F16)
        nbuf = sb("nbuf", [128, 512], U16)
        dpk = [sb(f"dpk{k}", [128, 1024], U16) for k in range(2)]
        dsc = [sb(f"dsc{k}", [128, 1024], F16) for k in range(2)]
        dslab = [sb(f"dslab{k}", [128, 1024], F16) for k in range(2)]
        dnb = sb("dnb", [128, 2048], U16)
        ps_g = self.psum(st, "ps_g", [128, 1024])
        ps_u = self.psum(st, "ps_u", [128, 1024])
        ps_g2 = self.psum(st, "ps_g2", [128, 1024])
        ps_u2 = self.psum(st, "ps_u2", [128, 1024])

        ld_gu = self.sem("ld_gu")
        ld_dn = self.sem("ld_dn")

        def load_pair(jjp, buf):
            t = None
            for w in range(4):
                ld_gu.inc(SY.dma_start(gpk[buf][:, w * 256:w * 256 + 128],
                          self.gup[w * 128:(w + 1) * 128, jjp * 128:(jjp + 1) * 128]), 16)
                ld_gu.inc(SY.dma_start(gpk[buf][:, w * 256 + 128:(w + 1) * 256],
                          self.gup[w * 128:(w + 1) * 128,
                                   IDIM + jjp * 128:IDIM + (jjp + 1) * 128]), 16)
                ld_gu.inc(SY.dma_start(gsc[buf][:, w * 256:w * 256 + 128],
                          self.gus[w * 128:(w + 1) * 128, jjp * 128:(jjp + 1) * 128]), 16)
                t = ld_gu.inc(SY.dma_start(
                    gsc[buf][:, w * 256 + 128:(w + 1) * 256],
                    self.gus[w * 128:(w + 1) * 128,
                             IDIM + jjp * 128:IDIM + (jjp + 1) * 128]), 16)
            return t

        def dq_ops(dst_ap, pk_ap, sc_ap, width, scratch):
            last = None
            for nb in range(4):
                amt, op = SHIFT_EM[nb]
                DV.tensor_scalar(scratch[:, 0:width], pk_ap, amt, 0x0E00, op, ALU.bitwise_and)
                sm, sa = SIGN_OP[nb]
                if sa is None:
                    DV.tensor_scalar(scratch[:, width:2 * width], pk_ap, sm, None,
                                     ALU.bitwise_and)
                else:
                    DV.tensor_scalar(scratch[:, width:2 * width], pk_ap, sm, sa,
                                     ALU.bitwise_and, ALU.logical_shift_left)
                DV.tensor_tensor(scratch[:, 0:width], scratch[:, 0:width],
                                 scratch[:, width:2 * width], ALU.bitwise_or)
                last = DV.scalar_tensor_tensor(
                    dst_ap[:, nb * width:(nb + 1) * width],
                    scratch[:, 0:width].bitcast(F16), 32768.0, sc_ap,
                    ALU.mult, ALU.mult)
            return last

        t_pair = [None, None]
        t_pair[0] = load_pair(0, 0)
        t_wdq_used = [None, None]
        t_h_done = None
        t_hb_used = None
        t_dn_ld = [None, None]
        t_dn_st = [None, None]
        dn_units = [(half, w2) for half in range(2) for w2 in range(11)]
        dn_i = 0

        def load_dn_unit(i):
            if i >= len(dn_units):
                return
            half, w2 = dn_units[i]
            par = i % 2
            wait(SY, t_dn_st[0])
            wait(SY, t_dn_st[1])
            ld_dn.inc(SY.dma_start(dpk[par][:, :], self.dnp[w2 * 128:(w2 + 1) * 128,
                                   half * 1024:(half + 1) * 1024]), 16)
            t_dn_ld[par] = ld_dn.inc(SY.dma_start(
                dsc[par][:, :], self.dns[w2 * 128:(w2 + 1) * 128,
                                         half * 1024:(half + 1) * 1024]), 16)

        def emit_dn_unit():
            nonlocal dn_i
            if dn_i >= len(dn_units):
                return
            half, w2 = dn_units[dn_i]
            par = dn_i % 2
            dn_i += 1
            load_dn_unit(dn_i)
            for nb in range(4):
                amt, op = SHIFT_EM[nb]
                wait(DV, t_dn_ld[par])
                wait(DV, t_dn_st[nb % 2])
                DV.tensor_scalar(dnb[:, 0:1024], dpk[par][:, :], amt, 0x0E00, op,
                                 ALU.bitwise_and)
                sm, sa = SIGN_OP[nb]
                if sa is None:
                    DV.tensor_scalar(dnb[:, 1024:2048], dpk[par][:, :], sm, None,
                                     ALU.bitwise_and)
                else:
                    DV.tensor_scalar(dnb[:, 1024:2048], dpk[par][:, :], sm, sa,
                                     ALU.bitwise_and, ALU.logical_shift_left)
                DV.tensor_tensor(dnb[:, 0:1024], dnb[:, 0:1024], dnb[:, 1024:2048],
                                 ALU.bitwise_or)
                t_sl = self.dv.inc(DV.scalar_tensor_tensor(
                    dslab[nb % 2][:, :], dnb[:, 0:1024].bitcast(F16), 32768.0,
                    dsc[par][:, :], ALU.mult, ALU.mult))
                wait(SY, t_sl)
                t_dn_st[nb % 2] = self.stq.inc(SY.dma_start(
                    self.d_wdn[half * IDIM + (4 * w2 + nb) * 128:
                               half * IDIM + (4 * w2 + nb + 1) * 128, :],
                    dslab[nb % 2][:, :]), 16)

        NB2 = max(1, T // 1024)
        BW = T // NB2

        def dequant_pair(jjp):
            buf = jjp % 2
            wait(DV, t_pair[buf])
            wait(DV, t_wdq_used[buf])
            idq = None
            for w in range(4):
                idq = dq_ops(wtil[buf][:, w * 1024:(w + 1) * 1024],
                             gpk[buf][:, w * 256:(w + 1) * 256],
                             gsc[buf][:, w * 256:(w + 1) * 256], 256, nbuf)
            return self.dv.inc(idq)

        t_dq = [dequant_pair(0), None]
        t_hmul = {}
        load_dn_unit(0)
        for jjp in range(NJJ):
            buf = jjp % 2
            if jjp + 1 < NJJ:
                t_pair[(jjp + 1) % 2] = load_pair(jjp + 1, (jjp + 1) % 2)
                t_dq[(jjp + 1) % 2] = dequant_pair(jjp + 1)
            wait(PE, t_dq[buf])
            wait(PE, t_xh2)
            for tb2 in range(NB2):
                pg = ps_g if tb2 % 2 == 0 else ps_g2
                pu = ps_u if tb2 % 2 == 0 else ps_u2
                wait(PE, t_h_done)
                ig = None
                for w in range(4):
                    for nb in range(4):
                        j = 4 * w + nb
                        base = (w * 4 + nb) * 256
                        for q in range(BW // 512):
                            PE.matmul(pg[:, q * 512:(q + 1) * 512],
                                      wtil[buf][:, base:base + 128],
                                      xh2[:, j * T + tb2 * BW + q * 512:
                                          j * T + tb2 * BW + (q + 1) * 512],
                                      start=(w == 0 and nb == 0), stop=(w == 3 and nb == 3))
                            ig = PE.matmul(pu[:, q * 512:(q + 1) * 512],
                                           wtil[buf][:, base + 128:base + 256],
                                           xh2[:, j * T + tb2 * BW + q * 512:
                                               j * T + tb2 * BW + (q + 1) * 512],
                                           start=(w == 0 and nb == 0), stop=(w == 3 and nb == 3))
                t_gu = self.pe.inc(ig)
                if tb2 == NB2 - 1:
                    t_wdq_used[buf] = t_gu
                wait(AC, t_gu)
                wait(AC, t_hmul.get(tb2))
                AC.activation(hsil[:, tb2 * BW:(tb2 + 1) * BW], pg[:, 0:BW], AF.Silu, scale=0.5)
                t_silu = self.ac.inc(AC.activation(up16[:, tb2 * BW:(tb2 + 1) * BW],
                                                   pu[:, 0:BW], AF.Copy,
                                                   scale=float(0.5 * 2 ** -7)))
                wait(GP, t_silu)
                wait(GP, t_hb_used)
                t_h_done = self.gp.inc(GP.tensor_tensor(hbuf[:, tb2 * BW:(tb2 + 1) * BW],
                                                        hsil[:, tb2 * BW:(tb2 + 1) * BW],
                                                        up16[:, tb2 * BW:(tb2 + 1) * BW],
                                                        ALU.mult))
                t_hmul[tb2] = t_h_done
            wait(SY, t_h_done)
            t_hb_used = self.stq.inc(SY.dma_start(self.d_h[jjp * 128:(jjp + 1) * 128, :],
                                                  hbuf[:, :]), 16)
            if jjp % 2 == 1:
                emit_dn_unit()
        while dn_i < len(dn_units):
            emit_dn_unit()

    def phase4(self, st):
        nc = self.nc
        PE, DV, AC, SY, GP = nc.tensor, nc.vector, nc.scalar, nc.sync, nc.gpsimd
        T = self.T
        NT = T // 128
        sb = lambda n_, s_, d_: self.sbuf(st, n_, s_, d_)
        slab = sb("slab", [128, NJJ * 1024], F16)
        hbuf2 = [sb(f"hb2_{k}", [128, NJJ * 128], F16) for k in range(2)]
        xpb = [sb(f"xpb{k}", [128, D], F16) for k in range(2)]
        y0 = sb("y0", [128, NT * 1024], F16)
        ysum = sb("ysum", [128, NT], F32)
        outb = [sb(f"outb{k}", [128, D], F32) for k in range(2)]
        sq2 = sb("sq2", [128, 1024], F32)
        rst = sb("rst", [128, 4], F32)
        ps_y = [self.psum(st, f"ps_y{i}", [128, 512]) for i in range(4)]

        ld_sl = self.sem("ld_sl")
        ld_h2 = self.sem("ld_h2")
        ld_xp2 = self.sem("ld_xp2")

        def load_h(tc, buf):
            return ld_h2.inc(SY.dma_start(
                hbuf2[buf][:, :].rearrange("p (jj t) -> p jj t", jj=NJJ),
                self.d_h.rearrange("(jj p) t -> p jj t", p=128)[:, :, tc * 128:(tc + 1) * 128]), 16)

        def load_xp(tc, buf):
            return ld_xp2.inc(SY.dma_start(xpb[buf][:, :],
                                           self.d_xp[tc * 128:(tc + 1) * 128, :]), 16)

        t_out_st = [None, None]
        t_sq_done = [None, None]
        t_h2_used = [None, None]
        t_xp_used = [None, None]
        for half in range(2):
            wait(SY, self.stq.tag())
            if half == 1:
                wait(SY, self.pe.tag())
            t_slab = ld_sl.inc(SY.dma_start(
                slab[:, :].rearrange("p (jj o) -> p jj o", jj=NJJ),
                self.d_wdn[half * IDIM:(half + 1) * IDIM, :]
                .rearrange("(jj p) o -> p jj o", p=128)), 16)
            t_h2 = [None, None]
            t_xpl = [None, None]
            t_h2[0] = load_h(0, 0)
            if half == 1:
                t_xpl[0] = load_xp(0, 0)
            t_y_evac = [None, None]
            for tc in range(NT):
                buf = tc % 2
                nb2 = (tc + 1) % 2
                if tc + 1 < NT:
                    wait(SY, t_h2_used[nb2])
                    t_h2[nb2] = load_h(tc + 1, nb2)
                    if half == 1:
                        wait(SY, t_xp_used[nb2])
                        t_xpl[nb2] = load_xp(tc + 1, nb2)
                wait(PE, t_slab)
                wait(PE, t_h2[buf])
                iy = None
                for sub in range(2):
                    pq = ps_y[2 * (tc % 2) + sub]
                    wait(PE, t_y_evac[tc % 2])
                    for jj in range(NJJ):
                        iy = PE.matmul(pq[:, :], hbuf2[buf][:, jj * 128:(jj + 1) * 128],
                                       slab[:, jj * 1024 + sub * 512:jj * 1024 + (sub + 1) * 512],
                                       start=(jj == 0), stop=(jj == NJJ - 1))
                t_ymm = self.pe.inc(iy)
                t_h2_used[buf] = t_ymm
                if half == 0:
                    wait(DV, t_ymm)
                    DV.tensor_scalar(y0[:, tc * 1024:tc * 1024 + 512],
                                     ps_y[2 * (tc % 2)][:, :], 0.25, None, ALU.mult)
                    t_ye = self.dv.inc(DV.tensor_scalar(
                        y0[:, tc * 1024 + 512:(tc + 1) * 1024],
                        ps_y[2 * (tc % 2) + 1][:, :], 0.25, None, ALU.mult))
                    t_y_evac[tc % 2] = t_ye
                    wait(AC, t_ye)
                    wait(AC, t_sq_done[tc % 2])
                    t_sq_done[tc % 2] = self.ac.inc(AC.activation(
                        sq2[:, :], y0[:, tc * 1024:(tc + 1) * 1024], AF.Square,
                        accum_out=ysum[:, tc:tc + 1]))
                else:
                    wait(AC, t_ymm)
                    wait(AC, t_y_evac[tc % 2])
                    wait(AC, t_y_evac[(tc + 1) % 2])
                    AC.activation(sq2[:, 0:512], ps_y[2 * (tc % 2)][:, :], AF.Square,
                                  scale=0.25, accum_out=rst[:, 0:1])
                    t_sqa = self.ac.inc(AC.activation(
                        sq2[:, 512:1024], ps_y[2 * (tc % 2) + 1][:, :], AF.Square,
                        scale=0.25, accum_out=rst[:, 1:2]))
                    t_sq_done[tc % 2] = t_sqa
                    wait(DV, t_sqa)
                    ta1 = self.dv.inc(DV.tensor_tensor(rst[:, 0:1], rst[:, 0:1],
                                                       rst[:, 1:2], ALU.add))
                    wait(DV, ta1)
                    ta2 = self.dv.inc(DV.tensor_tensor(rst[:, 0:1], rst[:, 0:1],
                                                       ysum[:, tc:tc + 1], ALU.add))
                    wait(DV, ta2)
                    t_vr = self.dv.inc(DV.tensor_scalar(rst[:, 0:1], rst[:, 0:1], 65536.0 / D,
                                                        FFN_EPS, ALU.mult, ALU.add))
                    wait(AC, t_vr)
                    tl1 = self.ac.inc(AC.activation(rst[:, 1:2], rst[:, 0:1], AF.Ln))
                    wait(AC, tl1)
                    t_rs = self.ac.inc(AC.activation(rst[:, 2:3], rst[:, 1:2], AF.Exp,
                                                     scale=-0.5))
                    wait(DV, t_rs)
                    DV.tensor_scalar(rst[:, 3:4], rst[:, 2:3], 64.0, None, ALU.mult)
                    ts2 = self.dv.inc(DV.tensor_scalar(rst[:, 1:2], rst[:, 2:3], 256.0,
                                                       None, ALU.mult))
                    wait(DV, ts2)
                    wait(DV, t_xpl[buf])
                    wait(DV, t_out_st[tc % 2])
                    ob = outb[tc % 2]
                    iv = None
                    for w in range(2):
                        for nb in range(4):
                            fb = 4 * w + nb
                            DV.scalar_tensor_tensor(
                                ob[:, 512 * w + nb:512 * (w + 1):4],
                                y0[:, tc * 1024 + fb * 128:tc * 1024 + (fb + 1) * 128],
                                rst[:, 1:2],
                                xpb[buf][:, fb * 128:(fb + 1) * 128],
                                ALU.mult, ALU.add)
                    for w in range(2):
                        for nb in range(4):
                            fb = 4 * w + nb
                            src = ps_y[2 * (tc % 2) + fb // 4]
                            iv = DV.scalar_tensor_tensor(
                                ob[:, 1024 + 512 * w + nb:1024 + 512 * (w + 1):4],
                                src[:, (fb % 4) * 128:(fb % 4 + 1) * 128],
                                rst[:, 3:4],
                                xpb[buf][:, 1024 + fb * 128:1024 + (fb + 1) * 128],
                                ALU.mult, ALU.add)
                    t_ob = self.dv.inc(iv)
                    t_y_evac[tc % 2] = t_ob
                    t_xp_used[buf] = t_ob
                    wait(SY, t_ob)
                    t_out_st[tc % 2] = self.sto.inc(
                        SY.dma_start(self.out[tc * 128:(tc + 1) * 128, :], ob[:, :]), 16)
        if "d_ysum" in self.dbg:
            dy = self.nc.dram_tensor("d_ysum", [128, NT], F32, kind="ExternalOutput").ap()
            wait(SY, self.dv.tag())
            self.stq.inc(SY.dma_start(dy[:, :], ysum[:, :]), 16)
        if "d_rst" in self.dbg:
            drs = self.nc.dram_tensor("d_rst", [128, 4], F32, kind="ExternalOutput").ap()
            wait(SY, self.dv.tag())
            self.stq.inc(SY.dma_start(drs[:, :], rst[:, :]), 16)
        if "d_y0" in self.dbg:
            dy0 = self.nc.dram_tensor("d_y0", [128, NT * 1024], F32, kind="ExternalOutput").ap()
            wait(SY, self.dv.tag())
            nc.vector.tensor_copy(sq2[:, :], y0[:, 0:1024])
            self.dv.inc(nc.vector.tensor_copy(sq2[:, :], y0[:, 0:1024]))
            for tcq in range(NT):
                iw = nc.vector.tensor_copy(sq2[:, :], y0[:, tcq * 1024:(tcq + 1) * 1024])
                tq = self.dv.inc(iw)
                wait(SY, tq)
                self.stq.inc(SY.dma_start(dy0[:, tcq * 1024:(tcq + 1) * 1024], sq2[:, :]), 16)

    def finale(self):
        nc = self.nc
        SY = nc.sync
        wait(SY, self.sto.tag())
        wait(SY, self.stq.tag())
        wait(SY, self.ld.tag())
        wait(nc.gpsimd, self.ldg.tag())


def build(n_cores=8, T=2048, dbg=()):
    mk = MK(n_cores=n_cores, T=T, dbg=dbg)
    nc = mk.build()
    return nc, mk


def kernel(**inputs):
    from concourse.bass_utils import run_bass_kernel_spmd
    n_cores, T = 8, 2048
    shared, per_core, pi = host_prep(inputs, n_cores, T)
    nc, _ = build(n_cores, T)
    in_maps = [{**shared, **pc} for pc in per_core]
    res = run_bass_kernel_spmd(nc, in_maps, core_ids=list(range(n_cores)))
    B, L = 2, 8192
    segs = n_cores // B
    out = np.empty((B, L, D), np.float32)
    for c in range(n_cores):
        b, seg = c // segs, c % segs
        out[b, seg * T:(seg + 1) * T, :] = res.results[c]["out"]
    return out
